# version 17
# baseline (speedup 1.0000x reference)
import contextlib

import numpy as np
import ml_dtypes

import concourse.bass as bass
import concourse.mybir as mybir
from concourse.bass_utils import run_bass_kernel_spmd

F32 = mybir.dt.float32
BF16 = mybir.dt.bfloat16
AF = mybir.ActivationFunctionType
ALU = mybir.AluOpType
AX = mybir.AxisListType

D = 2048
DFF = 8192
KD = D // 128
KFF = DFF // 128
OC_QKV = 12
OC_OUT = 4
OC_FC = 16
EPS1 = 1e-5
EPS2 = 1e-6
CLIPC = 7.8125e-9
WCNT = float(D * DFF)
N_P1 = 32
T2_SIGN = False

_CACHE = {}


class Sem:
    def __init__(self, h):
        self.h = h
        self.n = 0

    def bump(self, k=1):
        self.n += k
        return self.n


class DmaS:

    def __init__(self, sems):
        self.sems = sems
        self.i = 0

    def next(self):
        sem = self.sems[self.i % len(self.sems)]
        self.i += 1
        pre = sem.n
        post = sem.bump(16)
        return sem, pre, post


_BT_CAP = 4


def build(n_tc):
    NT = n_tc * 128
    BT = min(_BT_CAP, n_tc)
    NBLK = n_tc // BT
    nc = bass.Bass("TRN2", target_bir_lowering=False)

    x_in = nc.dram_tensor("x", [NT, D], F32, kind="ExternalInput")
    winT = nc.dram_tensor("winT", [D, 3 * D], F32, kind="ExternalInput")
    binb = nc.dram_tensor("binb", [3 * D], F32, kind="ExternalInput")
    woutT = nc.dram_tensor("woutT", [D, D], F32, kind="ExternalInput")
    boutb = nc.dram_tensor("boutb", [D], F32, kind="ExternalInput")
    wfcT = nc.dram_tensor("wfcT", [D, DFF], F32, kind="ExternalInput")
    wprojT = nc.dram_tensor("wprojT", [DFF, D], F32, kind="ExternalInput")
    tab = nc.dram_tensor("tab", [NT, 2048], F32, kind="ExternalInput")
    identf = nc.dram_tensor("identf", [128, 128], F32, kind="ExternalInput")
    out = nc.dram_tensor("out", [NT, D], F32, kind="ExternalOutput")
    cc_in = nc.dram_tensor("cc_in", [D], F32)
    cc_out = nc.dram_tensor("cc_out", [D], F32)
    wfc_t = nc.dram_tensor("wfc_t", [OC_FC, 128, DFF], BF16)
    wproj_t = nc.dram_tensor("wproj_t", [4, KFF, 128, 512], BF16)
    tern6 = nc.dram_tensor("tern6", [6], F32)
    tpre = nc.dram_tensor("tpre", [256], F32)
    wsh = nc.dram_tensor("wsh", [N_P1 * 128, 1024], F32, kind="ExternalInput")
    gam_in = nc.dram_tensor("gam_in", [2], F32)
    gam_out = nc.dram_tensor("gam_out", [2], F32)

    OPS = {k: [] for k in ("tensor", "vector", "scalar", "sync", "gpsimd")}

    def T(f):
        OPS["tensor"].append(f)

    CH = {}

    def V(f):
        tgt = CH.get("v", 0)

        def g(e, f=f, t=tgt):
            if t:
                e.wait_ge(s_ve.h, t)
            f(e).then_inc(s_ve.h, 1)
        OPS["vector"].append(g)
        CH["v"] = s_ve.bump(1)
        return CH["v"]

    def SC(f):
        tgt = CH.get("s", 0)

        def g(e, f=f, t=tgt):
            if t:
                e.wait_ge(s_se.h, t)
            f(e).then_inc(s_se.h, 1)
        OPS["scalar"].append(g)
        CH["s"] = s_se.bump(1)
        return CH["s"]

    def GC(f):
        tgt = CH.get("g", 0)

        def g(e, f=f, t=tgt):
            if t:
                e.wait_ge(s_gp.h, t)
            f(e).then_inc(s_gp.h, 1)
        OPS["gpsimd"].append(g)
        CH["g"] = s_gp.bump(1)
        return CH["g"]

    def SY(f):
        OPS["sync"].append(f)

    def G(f):
        OPS["gpsimd"].append(f)

    def wait(q, sem, tgt):
        if tgt is None or tgt <= 0:
            return
        OPS[q].append(lambda e, h=sem.h, t=tgt: e.wait_ge(h, t))

    def waitp(q, rec):
        if rec:
            wait(q, rec[0], rec[1])

    def dma(q, stream, fn):
        sem, pre, post = stream.next()
        wait(q, sem, pre)
        OPS[q].append(lambda e, f=fn, h=sem.h: f(e).then_inc(h, 16))
        return (sem, post)

    with contextlib.ExitStack() as es:
        E = es.enter_context
        AR = 16384
        arnA = E(nc.sbuf_tensor("arnA", [128, AR], BF16))
        arnB = E(nc.sbuf_tensor("arnB", [128, AR], BF16))
        arnE = E(nc.sbuf_tensor("arnE", [128, AR], BF16))
        arnF = E(nc.sbuf_tensor("arnF", [128, AR], BF16))
        big = E(nc.sbuf_tensor("big", [128, AR], BF16))
        sc = E(nc.sbuf_tensor("sc", [128, 4096], F32))
        scb = sc.ap().bitcast(BF16)
        tabb = E(nc.sbuf_tensor("tabb", [128, 2048], F32))
        tabbb = tabb.ap().bitcast(BF16)
        ternw = E(nc.sbuf_tensor("ternw", [128, 2048], F32))
        ternt = E(nc.sbuf_tensor("ternt", [128, 2048], BF16))
        ternc2 = E(nc.sbuf_tensor("ternc2", [128, 2048], BF16))
        biasb = E(nc.sbuf_tensor("biasb", [128, 1024], F32))
        idf = E(nc.sbuf_tensor("idf", [128, 128], F32))
        idb = E(nc.sbuf_tensor("idb", [128, 128], BF16))
        onesb = E(nc.sbuf_tensor("onesb", [128, 8], BF16))
        st_ = E(nc.sbuf_tensor("stats", [128, 160], F32))
        asumP = E(nc.sbuf_tensor("asumP", [128, 256], F32))
        a_acc = E(nc.sbuf_tensor("a_acc", [128, 16], F32))
        arow = E(nc.sbuf_tensor("arow", [128, 128], F32))
        trow = E(nc.sbuf_tensor("trow", [128, 136], F32))
        ternbc = E(nc.sbuf_tensor("ternbc", [128, 8], F32))
        hsum = E(nc.sbuf_tensor("hsum", [128, 64], F32))

        pb = [E(nc.psum_tensor("pb%d" % i, [128, 512], F32)) for i in range(5)]
        pb5 = E(nc.psum_tensor("pb5", [128, 512], F32))
        pA = E(nc.psum_tensor("pA", [128, 512], F32))
        ptr = E(nc.psum_tensor("ptr", [128, 1024], BF16))
        ptr2 = pb5.ap().bitcast(BF16)

        SL = DmaS([Sem(E(nc.semaphore("sl%d" % i))) for i in range(2)])
        TW = DmaS([Sem(E(nc.semaphore("tw%d" % i))) for i in range(2)])
        TW2 = DmaS([Sem(E(nc.semaphore("tx%d" % i))) for i in range(4)])
        ST = DmaS([Sem(E(nc.semaphore("st%d" % i))) for i in range(4)])
        SG = DmaS([Sem(E(nc.semaphore("sg%d" % i))) for i in range(2)])
        SG2 = DmaS([Sem(E(nc.semaphore("sh%d" % i))) for i in range(2)])
        WL = DmaS([Sem(E(nc.semaphore("wl%d" % i))) for i in range(2)])
        OU = DmaS([Sem(E(nc.semaphore("ou%d" % i))) for i in range(2)])
        s_cc = Sem(E(nc.semaphore("s_cc")))
        s_gp = Sem(E(nc.semaphore("s_gp")))
        s_mm = Sem(E(nc.semaphore("s_mm")))
        s_mp = Sem(E(nc.semaphore("s_mp")))
        s_tp = Sem(E(nc.semaphore("s_tp")))
        s_ve = Sem(E(nc.semaphore("s_ve")))
        s_se = Sem(E(nc.semaphore("s_se")))

        blk = E(nc.Block())

        def stv(i):
            return st_[:, i:i + 1]

        C_M1, C_R1 = 0, 16
        C_T = 32
        C_KS, C_RINV = 40, 41
        C_M, C_R = 42, 43
        C_G, C_CL, C_NCL = 44, 45, 46
        C_S2 = 48

        bank_free = {}

        def bank_wait(b):
            t = bank_free.get(b)
            if t:
                wait("tensor", t[0], t[1])

        trp_free = {0: None, 1: None}
        tr_par = [0]

        def tr_group(src_ap_fn, out_ap):
            p = tr_par[0] % 2
            tr_par[0] += 1
            dst = ptr if p == 0 else ptr2
            t = trp_free[p]
            if t:
                wait("tensor", t[0], t[1])
            for j in range(8):
                def tr(e, j_=j, d=dst, fn=src_ap_fn, last=(j == 7)):
                    inst = e.transpose(d[:, j_ * 128:(j_ + 1) * 128], fn(j_), idb[:])
                    if last:
                        inst.then_inc(s_tp.h, 1)
                T(tr)
            tp_t = s_tp.bump(1)
            wait("vector", s_tp, tp_t)
            V(lambda e, d=dst, o=out_ap: e.tensor_copy(out=o, in_=d[:, 0:1024])
              )
            ev_t = s_ve.n
            trp_free[p] = (s_ve, ev_t)
            return tp_t, ev_t

        def ln_stats_from(sum_col, ssq_col, eps, inv_n):
            V(lambda e: e.tensor_scalar(out=stv(C_M), in0=stv(sum_col),
                                        scalar1=inv_n, scalar2=None, op0=ALU.mult))
            V(lambda e: e.tensor_scalar(out=stv(C_T + 2), in0=stv(ssq_col),
                                        scalar1=inv_n, scalar2=None, op0=ALU.mult))
            V(lambda e: e.tensor_tensor(out=stv(C_T + 3), in0=stv(C_M), in1=stv(C_M),
                                        op=ALU.mult))
            V(lambda e: e.tensor_sub(out=stv(C_T + 4), in0=stv(C_T + 2),
                                     in1=stv(C_T + 3)))
            return s_ve.n

        sp_guard = [0]

        def stats_pair(src, soff):
            wait("scalar", s_ve, sp_guard[0])
            for j in range(2):
                SC(lambda e, o=soff + j * 1024, j_=j, a=src: e.activation(
                    out=ternc2[:, 1024:2048], in_=a[:, o:o + 1024], func=AF.Copy,
                    accum_out=stv(52 + j_)))
            for j in range(2):
                SC(lambda e, o=soff + j * 1024, j_=j, a=src: e.activation(
                    out=ternc2[:, 1024:2048], in_=a[:, o:o + 1024], func=AF.Square,
                    accum_out=stv(56 + j_)))
            t = s_se.n
            wait("vector", s_se, t)
            V(lambda e: e.tensor_add(out=stv(C_T), in0=stv(52), in1=stv(53)))
            V(lambda e: e.tensor_add(out=stv(C_T + 1), in0=stv(56), in1=stv(57)))
            sp_guard[0] = s_ve.n

        def ln_sqrt_recip(var_t, eps):
            epsc = 150 if eps == EPS1 else 151
            wait("scalar", s_ve, var_t)
            SC(lambda e, b=epsc: e.activation(out=stv(C_T + 5), in_=stv(C_T + 4),
                                             func=AF.Sqrt, bias=stv(b)))
            sq_t = s_se.n
            wait("vector", s_se, sq_t)
            V(lambda e: e.reciprocal(out=stv(C_R), in_=stv(C_T + 5)))

        T_IDF = dma("sync", SL, lambda e: e.dma_start(out=idf[:], in_=identf[:, :]))
        V(lambda e: e.memset(onesb[:], 1.0))
        V(lambda e: e.memset(st_[:, 150:151], EPS1))
        V(lambda e: e.memset(st_[:, 151:152], EPS2))
        waitp("vector", T_IDF)
        V(lambda e: e.tensor_copy(out=idb[:], in_=idf[:]))
        T_IDB = s_ve.n

        p1_ld, p1_dn = [0], [0]
        t2_ld_n, t2_dn = [0], [0]
        p1_ldrec, t2_ldrec = {}, {}
        t1_acc_t = {}
        st_rec = {}
        t2_strec = {}
        ternbc_rec = {}
        gates = {}

        t2_4s = [False]
        t2_slot_n = [0, 0]
        t2_last_use = {}
        t2_scrx = [0]
        t2_sign = [0]

        def t2_slot_ap(name):
            if name == "t0":
                return ternt[:, 0:1024]
            if name == "t1":
                return ternt[:, 1024:2048]
            if name == "sA":
                return scb[:, 4096:5120]
            return scb[:, 5120:6144]

        def t2_sd(i):
            if i < 128:
                ocq, kp = divmod(i, 16)
                src = wfcT[kp * 128:(kp + 1) * 128, ocq * 1024:(ocq + 1) * 1024]
                dst = wfc_t[2 * ocq:2 * ocq + 2, :, kp * 512:(kp + 1) * 512] \
                    .rearrange("o p c -> p o c")
                return src, dst, 1
            cq, r = divmod(i - 128, 64)
            src = wprojT[r * 128:(r + 1) * 128, cq * 1024:(cq + 1) * 1024]
            dst = wproj_t[2 * cq:2 * cq + 2, r, :, :].rearrange("e p c -> p e c")
            return src, dst, 4

        def tern_step(nn, which="both"):
            if which in ("p1", "both"):
                for _ in range(min(nn, N_P1 - p1_dn[0])):
                    i = p1_ld[0]
                    if i < N_P1 and i - p1_dn[0] < 2:
                        p1_ldrec[i] = dma(
                            "scalar", TW,
                            lambda e, ss=wsh[i * 128:(i + 1) * 128, :],
                            h=(i % 2) * 1024:
                            e.dma_start(out=ternw[:, h:h + 1024], in_=ss))
                        p1_ld[0] += 1
                    if p1_dn[0] < p1_ld[0]:
                        i = p1_dn[0]
                        waitp("scalar", p1_ldrec[i])
                        t1_acc_t[i] = SC(
                            lambda e, h=(i % 2) * 1024, i_=i: e.activation(
                                out=ternc2[:, 1024:2048], in_=ternw[:, h:h + 1024],
                                func=AF.Abs, accum_out=asumP[:, i_:i_ + 1]))
                        p1_dn[0] += 1
            if which in ("t2", "both"):
                for _ in range(min(nn, 256 - t2_dn[0])):
                    i = t2_ld_n[0]
                    if i < 256 and i - t2_dn[0] < 4:
                        if ((i // 128) not in ternbc_rec or "yt" not in gates
                                or p1_dn[0] < N_P1):
                            break
                        q2 = "sync" if i % 2 == 0 else "scalar"
                        if i < 4:
                            wait(q2, s_ve, gates["yt"])
                            wait(q2, s_se, t1_acc_t[N_P1 - 1])
                        if i >= 4:
                            wait(q2, s_ve, gates[("t2", i - 4)])
                        sl4 = i % 4
                        buf = ternw if sl4 < 2 else tabb
                        t2_ldrec[i] = dma(
                            q2, TW2,
                            lambda e, ss=t2_sd(i)[0], b=buf, h=(i % 2) * 1024:
                            e.dma_start(out=b[:, h:h + 1024], in_=ss))
                        t2_ld_n[0] += 1
                    if t2_dn[0] < t2_ld_n[0]:
                        i = t2_dn[0]
                        if i == 0:
                            waitp("vector", ternbc_rec[0])
                        if i == 128:
                            waitp("vector", ternbc_rec[1])
                        waitp("vector", t2_ldrec[i])
                        if t2_4s[0]:
                            sl = ("t0", "t1", "sA", "sB")[t2_slot_n[0] % 4]
                            t2_slot_n[0] += 1
                        else:
                            sl = ("t0", "t1")[t2_slot_n[1] % 2]
                            t2_slot_n[1] += 1
                        waitp("vector", t2_last_use.get(sl))
                        oap = t2_slot_ap(sl)
                        _, dst, thc = t2_sd(i)
                        buf = ternw if (i % 4) < 2 else tabb
                        h = (i % 2) * 1024
                        wap = buf[:, h:h + 1024]
                        if t2_4s[0] and i % 2 == 1 and T2_SIGN:
                            if t2_scrx[0]:
                                wait("scalar", s_ve, t2_scrx[0])
                            waitp("scalar", t2_ldrec[i])
                            waitp("scalar", ternbc_rec[i // 128])
                            waitp("scalar", t2_last_use.get(sl))
                            SC(lambda e, w=wap, c=thc: e.activation(
                                out=scb[:, 6144:7168], in_=w, func=AF.Sign,
                                bias=ternbc[:, c + 1:c + 2]))
                            SC(lambda e, w=wap, c=thc, o=oap: e.activation(
                                out=o, in_=w, func=AF.Sign,
                                bias=ternbc[:, c:c + 1]))
                            sa = s_se.n
                            t2_sign[0] = sa
                            wait("vector", s_se, sa)
                            gt = V(lambda e, o=oap: e.tensor_add(
                                out=o, in0=o, in1=scb[:, 6144:7168]))
                            t2_scrx[0] = gt
                        else:
                            V(lambda e, w=wap, c=thc: e.tensor_scalar(
                                out=ternc2[:, 0:1024], in0=w,
                                scalar1=ternbc[:, c + 1:c + 2], scalar2=2.0,
                                op0=ALU.is_lt, op1=ALU.mult))
                            V(lambda e, w=wap, c=thc, o=oap: e.tensor_scalar(
                                out=o, in0=w,
                                scalar1=ternbc[:, c:c + 1], scalar2=2.0,
                                op0=ALU.is_gt, op1=ALU.mult))
                            gt = V(lambda e, o=oap: e.tensor_sub(
                                out=o, in0=o, in1=ternc2[:, 0:1024]))
                        gates[("t2", i)] = gt
                        wait("gpsimd", s_ve, gt)
                        rec = dma("gpsimd", ST, lambda e, o=oap, d=dst:
                                  e.dma_start(out=d, in_=o))
                        st_rec[i] = rec
                        t2_strec[i] = rec
                        t2_last_use[sl] = rec
                        t2_dn[0] += 1

        def t1_step(nn):
            tern_step(nn, "p1")

        def t2_step(nn):
            tern_step(nn, "both")
        t2_done = t2_dn
        N_T2 = 256

        x_slot_free = {0: None, 1: None}
        for tc in range(n_tc):
            slot = (tc % 2) * 2048
            if x_slot_free[tc % 2]:
                wait("sync", s_ve, x_slot_free[tc % 2])
            x_rec = dma("sync", SL, lambda e, t=tc, s=slot: e.dma_start(
                out=sc[:, s:s + 2048], in_=x_in[t * 128:(t + 1) * 128, :]))
            waitp("scalar", x_rec)
            eoff = tc * 2048
            SC(lambda e, s=slot, d=eoff: e.activation(
                out=arnE[:, d:d + 2048], in_=sc[:, s:s + 2048], func=AF.Copy,
                accum_out=stv(C_T)))
            SC(lambda e, s=slot, d=eoff: e.activation(
                out=arnE[:, d:d + 2048], in_=sc[:, s:s + 2048], func=AF.Square,
                accum_out=stv(C_T + 1)))
            acc_t = s_se.n
            wait("vector", s_se, acc_t)
            var_t = ln_stats_from(C_T, C_T + 1, EPS1, 1.0 / D)
            ln_sqrt_recip(var_t, EPS1)
            V(lambda e, t=tc: e.tensor_copy(out=stv(C_M1 + t), in_=stv(C_M)))
            V(lambda e, t=tc: e.tensor_copy(out=stv(C_R1 + t), in_=stv(C_R)))
            V(lambda e, t=tc, s=slot, d=eoff: e.tensor_scalar(
                out=arnE[:, d:d + 2048], in0=sc[:, s:s + 2048],
                scalar1=stv(C_M), scalar2=stv(C_R),
                op0=ALU.subtract, op1=ALU.mult))
            u_t = s_ve.n
            x_slot_free[tc % 2] = u_t

            wait("tensor", s_ve, max(u_t, T_IDB))
            for g in range(2):
                tr_group(
                    (lambda j, t=tc, g_=g: arnE[:, t * 2048 + (g_ * 8 + j) * 128:
                                                t * 2048 + (g_ * 8 + j + 1) * 128]),
                    arnA[:, tc * 2048 + g * 1024: tc * 2048 + (g + 1) * 1024])
        ALL_XT = s_ve.n

        def slab_ap(slot, k):
            return arnB[:, slot * 8192 + k * 512: slot * 8192 + (k + 1) * 512]

        qkv_grp, qkv_ev = {}, {}
        oc_last = {}
        rope_done, fmkv_done, a_grp = {}, {}, {}
        tabload_t, bias_t, slab_t = {}, {}, {}
        grp = [0]

        def emit_slab(oc, reuse_wait):
            if reuse_wait:
                wait("gpsimd", reuse_wait[0], reuse_wait[1])
            if oc < OC_QKV:
                src = winT[:, oc * 512:(oc + 1) * 512]
            else:
                o = oc - OC_QKV
                src = woutT[:, o * 512:(o + 1) * 512]
            src = src.rearrange("(k p) o -> p k o", p=128)
            dst = arnB[:, (oc % 2) * 8192:(oc % 2) * 8192 + 8192].rearrange(
                "p (k o) -> p k o", k=KD)
            slab_t[oc] = dma("gpsimd", WL,
                             lambda e, s=src, d=dst: e.dma_start(out=d, in_=s))

        def emit_bias(oc, reuse_wait):
            if reuse_wait:
                wait("sync", reuse_wait[0], reuse_wait[1])
            src_t = binb if oc < OC_QKV else boutb
            o = oc if oc < OC_QKV else oc - OC_QKV
            bias_t[oc] = dma("sync", SL, lambda e, s=src_t, o_=o,
                             sl=(oc % 2) * 512: e.dma_start(
                out=biasb[:, sl:sl + 512],
                in_=s[o_ * 512:(o_ + 1) * 512][None, :].broadcast_to([128, 512])))

        def emit_table(r, tc, kind):
            if kind == "k":
                if r >= 2:
                    wait("sync", s_ve, rope_done[r - 2])
                par = (tc % 2) * 1024
                tabload_t[r] = dma("sync", SL, lambda e, t=tc, p=par: e.dma_start(
                    out=tabb[:, p:p + 1024].rearrange("p (f c) -> p f c", f=2),
                    in_=tab[t * 128:(t + 1) * 128, :]
                    .rearrange("p (f c) -> p f c", f=4)[:, 0::2, :]))
            else:
                if r >= 1:
                    wait("sync", s_ve, rope_done[r - 1])
                tabload_t[r] = dma("sync", SL, lambda e, t=tc: e.dma_start(
                    out=tabb[:], in_=tab[t * 128:(t + 1) * 128, :]))

        def rope14(base_arn, tcoff, table_t):
            r = [sc[:, i * 512:(i + 1) * 512] for i in range(6)]
            qe = base_arn[:, tcoff:tcoff + 512]
            qo = base_arn[:, tcoff + 512:tcoff + 1024]
            ch, cl = tabb[:, 0:512], tabb[:, 512:1024]
            sh, sl_ = tabb[:, 1024:1536], tabb[:, 1536:2048]
            waitp("vector", table_t)
            V(lambda e: e.tensor_mul(out=r[0], in0=qe, in1=sh))
            V(lambda e: e.tensor_mul(out=r[1], in0=qe, in1=sl_))
            V(lambda e: e.tensor_mul(out=r[2], in0=qe, in1=ch))
            V(lambda e: e.tensor_mul(out=r[3], in0=qe, in1=cl))
            V(lambda e: e.tensor_mul(out=r[4], in0=qo, in1=sh))
            V(lambda e: e.tensor_sub(out=r[5], in0=r[2], in1=r[4]))
            V(lambda e: e.tensor_mul(out=r[4], in0=qo, in1=sl_))
            V(lambda e: e.tensor_add(out=r[2], in0=r[5], in1=r[3]))
            V(lambda e: e.tensor_sub(out=qe, in0=r[2], in1=r[4]))
            V(lambda e: e.tensor_mul(out=r[3], in0=qo, in1=ch))
            V(lambda e: e.tensor_mul(out=r[5], in0=qo, in1=cl))
            V(lambda e: e.tensor_add(out=r[2], in0=r[3], in1=r[0]))
            V(lambda e: e.tensor_add(out=r[4], in0=r[5], in1=r[1]))
            V(lambda e: e.tensor_add(out=qo, in0=r[2], in1=r[4]))
            return s_ve.n

        emit_slab(0, None)
        emit_slab(1, None)
        emit_bias(0, None)
        emit_bias(1, None)

        def xT_ap(k, tc):
            off = (tc * KD + k) * 128
            return arnA[:, off:off + 128]

        for oc in range(OC_QKV):
            for tc in range(n_tc):
                b = grp[0] % 5
                waitp("tensor", slab_t[oc])
                if oc == 0 and tc == 0:
                    wait("tensor", s_ve, ALL_XT)
                bank_wait(b)
                for k in range(KD):
                    def mm(e, o=oc, t=tc, k_=k, b_=b, last=(k == KD - 1)):
                        inst = e.matmul(pb[b_][:, 0:512], xT_ap(k_, t),
                                        slab_ap(o % 2, k_), start=(k_ == 0), stop=last)
                        if last:
                            inst.then_inc(s_mm.h, 1)
                    T(mm)
                qkv_grp[(oc, tc)] = s_mm.bump(1)
                grp[0] += 1

                wait("vector", s_mm, qkv_grp[(oc, tc)])
                waitp("vector", bias_t[oc])
                if oc < 4:
                    dst, doff = arnE, tc * 2048 + oc * 512
                elif oc < 8:
                    dst, doff = arnF, tc * 2048 + (oc - 4) * 512
                else:
                    dst, doff = big, tc * 2048 + (oc - 8) * 512
                def evac(e, b_=b, d=dst, o_=doff, ob=(oc % 2) * 512):
                    return e.tensor_tensor(out=d[:, o_:o_ + 512], in0=pb[b_][:, 0:512],
                                           in1=biasb[:, ob:ob + 512], op=ALU.add)
                V(evac)
                qkv_ev[(oc, tc)] = s_ve.n
                bank_free[b] = (s_ve, qkv_ev[(oc, tc)])

                if oc == 7:
                    emit_table(tc, tc, "k")
                    par = (tc % 2) * 1024
                    r0 = sc[:, 0:512]
                    r1 = sc[:, 512:1024]
                    r2 = sc[:, 1024:1536]
                    ke = arnF[:, tc * 2048:tc * 2048 + 512]
                    ko = arnF[:, tc * 2048 + 512:tc * 2048 + 1024]
                    ch = tabb[:, par:par + 512]
                    sh = tabb[:, par + 512:par + 1024]
                    waitp("vector", tabload_t[tc])
                    V(lambda e, ke=ke, sh=sh: e.tensor_mul(out=r0, in0=ke, in1=sh))
                    V(lambda e, ke=ke, ch=ch: e.tensor_mul(out=r1, in0=ke, in1=ch))
                    V(lambda e, ko=ko, sh=sh: e.tensor_mul(out=r2, in0=ko, in1=sh))
                    V(lambda e, ke=ke: e.tensor_sub(out=ke, in0=r1, in1=r2))
                    V(lambda e, ko=ko, ch=ch: e.tensor_mul(out=r1, in0=ko, in1=ch))
                    rope_done[tc] = V(lambda e, ko=ko: e.tensor_add(out=ko, in0=r1, in1=r0))
                    V(lambda e, t=tc: e.scalar_tensor_tensor(
                        out=scb[:, 4096:6144],
                        in0=arnF[:, t * 2048:t * 2048 + 2048],
                        scalar=1.0, in1=arnF[:, t * 2048:t * 2048 + 2048],
                        op0=ALU.mult, op1=ALU.mult, accum_out=stv(C_KS)))
                    ks_t = s_ve.n
                    wait("scalar", s_ve, ks_t)
                    SC(lambda e: e.activation(out=stv(C_T + 6), in_=stv(C_KS),
                                              func=AF.Sqrt))
                    kn_t = s_se.n
                    wait("vector", s_se, kn_t)
                    V(lambda e: e.tensor_scalar(out=stv(C_T + 7), in0=stv(C_T + 6),
                                                scalar1=1e-12, scalar2=None,
                                                op0=ALU.max))
                    V(lambda e: e.reciprocal(out=stv(C_RINV), in_=stv(C_T + 7)))
                    for j in range(4):
                        def fm(e, t=tc, j_=j):
                            return e.scalar_tensor_tensor(
                                out=arnE[:, t * 2048 + j_ * 512:t * 2048 + (j_ + 1) * 512],
                                in0=arnF[:, t * 2048 + j_ * 512:t * 2048 + (j_ + 1) * 512],
                                scalar=stv(C_RINV),
                                in1=arnE[:, t * 2048 + j_ * 512:t * 2048 + (j_ + 1) * 512],
                                op0=ALU.mult, op1=ALU.mult)
                        V(fm)
                    fmkv_done[tc] = s_ve.n

                if oc == 8:
                    wait("tensor", s_ve, fmkv_done[tc])
                    for c in range(KD):
                        def am(e, c_=c, t=tc,
                               st=(tc == 0 and c == 0),
                               sp=(tc == n_tc - 1 and c == KD - 1),
                               last=(c == KD - 1)):
                            inst = e.matmul(pA[:, c_:c_ + 1],
                                            arnE[:, t * 2048 + c_ * 128:
                                                 t * 2048 + (c_ + 1) * 128],
                                            onesb[:, 0:1], start=st, stop=sp)
                            if last:
                                inst.then_inc(s_mm.h, 1)
                        T(am)
                    a_grp[tc] = s_mm.bump(1)

                if oc == 9:
                    emit_table(n_tc + tc, tc, "q")
                    rope_done[n_tc + tc] = rope14(big, tc * 2048, tabload_t[n_tc + tc])

                t1_step(3)

            oc_last[oc] = qkv_grp[(oc, n_tc - 1)]
            if oc + 2 < OC_QKV:
                emit_slab(oc + 2, (s_mm, oc_last[oc]))
                emit_bias(oc + 2, (s_ve, qkv_ev[(oc, n_tc - 1)]))
            if oc == 9:
                pass
            if oc == 10:
                emit_slab(12, (s_mm, oc_last[10]))
            if oc == 11:
                emit_slab(13, (s_mm, oc_last[11]))

            if oc == 8:
                wait("vector", s_mm, a_grp[n_tc - 1])
                V(lambda e: e.tensor_copy(out=a_acc[:], in_=pA[:, 0:16])
                  )
                aev_t = s_ve.n
                wait("tensor", s_ve, aev_t)
                T(lambda e: e.transpose(pA[0:16, 0:128], a_acc[:], idf[:])
                  .then_inc(s_tp.h, 1))
                atr_t = s_tp.bump(1)
                wait("vector", s_tp, atr_t)
                V(lambda e: e.tensor_copy(out=arow[0:16, 0:128], in_=pA[0:16, 0:128])
                  )
                arow_t = s_ve.n
                wait("sync", s_ve, arow_t)
                ccin_t = dma("sync", SL, lambda e: e.dma_start(
                    out=cc_in.ap().rearrange("(c p) -> c p", c=16),
                    in_=arow[0:16, 0:128]))

        waitp("gpsimd", ccin_t)
        G(lambda e: e.collective_compute(
            "AllReduce", ALU.add, replica_groups=[[0, 1, 2, 3], [4, 5, 6, 7]],
            ins=[cc_in[:]], outs=[cc_out[:]]).then_inc(s_cc.h, 1))
        cc_t = s_cc.bump(1)

        wait("vector", s_se, t1_acc_t[N_P1 - 1])
        V(lambda e: e.reduce_sum(out=stv(C_S2), in_=asumP[:, 0:16], axis=AX.X))
        V(lambda e: e.reduce_sum(out=stv(C_S2 + 1), in_=asumP[:, 16:32],
                                 axis=AX.X))
        gsum_t = s_ve.n
        wait("sync", s_ve, gsum_t)
        tp1 = dma("sync", SL, lambda e: e.dma_start(
            out=tpre.ap()[0:128].rearrange("(p j) -> p j", j=1),
            in_=st_[:, C_S2:C_S2 + 1]))
        waitp("sync", tp1)
        tp2 = dma("sync", SL, lambda e: e.dma_start(
            out=tpre.ap()[128:256].rearrange("(p j) -> p j", j=1),
            in_=st_[:, C_S2 + 1:C_S2 + 2]))
        waitp("sync", tp2)
        tp3 = dma("sync", SL, lambda e: e.dma_start(
            out=trow[0:2, 0:128],
            in_=tpre.ap().rearrange("(j p) -> j p", j=2)))
        waitp("vector", tp3)
        V(lambda e: e.reduce_sum(out=trow[0:2, 128:129], in_=trow[0:2, 0:128],
                                 axis=AX.X))
        gprt_t = s_ve.n
        wait("sync", s_ve, gprt_t)
        gi_t = dma("sync", SL, lambda e: e.dma_start(
            out=gam_in.ap().rearrange("(p j) -> p j", j=1),
            in_=trow[0:2, 128:129]))
        waitp("gpsimd", gi_t)
        G(lambda e: e.collective_compute(
            "AllReduce", ALU.add,
            replica_groups=[[0, 1, 2, 3, 4, 5, 6, 7]],
            ins=[gam_in[:]], outs=[gam_out[:]]).then_inc(s_cc.h, 1))
        cc2_t = s_cc.bump(1)
        wait("sync", s_cc, cc_t)
        wait("sync", s_ve, rope_done[2 * n_tc - 1])
        abc_t = dma("sync", SL, lambda e: e.dma_start(
            out=tabb[:], in_=cc_out[None, :].broadcast_to([128, 2048])))

        y_tp = {}
        for tc in range(n_tc):
            waitp("vector", abc_t)
            wait("vector", s_ve, qkv_ev[(11, tc)])
            if tc >= 2:
                wait("vector", s_tp, y_tp[tc - 2])
            ybuf = (tc % 2) * 2048
            for j in range(4):
                def ymul(e, t=tc, j_=j, yb=ybuf):
                    return e.tensor_mul(
                        out=scb[:, yb + j_ * 512:yb + (j_ + 1) * 512],
                        in0=tabb[:, j_ * 512:(j_ + 1) * 512],
                        in1=big[:, t * 2048 + j_ * 512:t * 2048 + (j_ + 1) * 512])
                V(ymul)
            y_t = s_ve.n
            wait("tensor", s_ve, y_t)
            for g in range(2):
                tp_t, _ = tr_group(
                    (lambda j, yb=ybuf, g_=g: scb[:, yb + (g_ * 8 + j) * 128:
                                                  yb + (g_ * 8 + j + 1) * 128]),
                    arnA[:, tc * 2048 + g * 1024: tc * 2048 + (g + 1) * 1024])
                if g == 1:
                    y_tp[tc] = tp_t
        ALL_YT = s_ve.n
        gates["yt"] = ALL_YT
        MM_B1 = s_mm.n
        emit_slab(14, (s_mm, MM_B1 + n_tc))
        emit_slab(15, (s_mm, MM_B1 + 2 * n_tc))

        emit_bias(OC_QKV + 0, (s_ve, qkv_ev[(10, n_tc - 1)]))
        emit_bias(OC_QKV + 1, (s_ve, qkv_ev[(11, n_tc - 1)]))
        b1_grp, b1_ev = {}, {}
        for oc in range(OC_OUT):
            for tc in range(n_tc):
                b = grp[0] % 5
                waitp("tensor", slab_t[OC_QKV + oc])
                if oc == 0 and tc == 0:
                    wait("tensor", s_ve, ALL_YT)
                bank_wait(b)
                for k in range(KD):
                    def mm2(e, t=tc, k_=k, b_=b, o=oc, last=(k == KD - 1)):
                        inst = e.matmul(pb[b_][:, 0:512], xT_ap(k_, t),
                                        slab_ap((OC_QKV + o) % 2, k_),
                                        start=(k_ == 0), stop=last)
                        if last:
                            inst.then_inc(s_mm.h, 1)
                    T(mm2)
                b1_grp[(oc, tc)] = s_mm.bump(1)
                grp[0] += 1
                wait("vector", s_mm, b1_grp[(oc, tc)])
                waitp("vector", bias_t[OC_QKV + oc])
                def evac2(e, b_=b, t=tc, o=oc, ob=(oc % 2) * 512):
                    return e.tensor_tensor(
                        out=big[:, t * 2048 + o * 512:t * 2048 + (o + 1) * 512],
                        in0=pb[b_][:, 0:512], in1=biasb[:, ob:ob + 512], op=ALU.add)
                V(evac2)
                b1_ev[(oc, tc)] = s_ve.n
                bank_free[b] = (s_ve, b1_ev[(oc, tc)])
                tern_step(3, "p1")
            oc_last[OC_QKV + oc] = b1_grp[(oc, n_tc - 1)]
            if oc == 0:
                emit_bias(OC_QKV + 2, (s_ve, b1_ev[(0, n_tc - 1)]))
            if oc == 1:
                emit_bias(OC_QKV + 3, (s_ve, b1_ev[(1, n_tc - 1)]))

        p5_done = {}

        def emit_p5(tc):
            slot = (tc % 2) * 2048
            oth = 2048 - slot
            if tc >= 1:
                wait("sync", s_ve, p5_done[tc - 1])
            else:
                wait("sync", s_ve, ALL_YT)
                wait("sync", s_tp, y_tp[n_tc - 1])
            pb0 = (tc // BT) * BT
            waitp("sync", out_t.get(pb0 - 1))
            waitp("sync", out_t.get(pb0 - 2))
            waitp("sync", t2_last_use.get("sA"))
            waitp("sync", t2_last_use.get("sB"))
            waitp("vector", t2_last_use.get("sA"))
            waitp("vector", t2_last_use.get("sB"))
            wait("sync", s_se, t2_sign[0])
            wait("vector", s_se, t2_sign[0])
            xre_t = dma("sync", SL, lambda e, t=tc, s=slot: e.dma_start(
                out=sc[:, s:s + 2048], in_=x_in[t * 128:(t + 1) * 128, :]))
            waitp("vector", xre_t)
            waitp("vector", out_t.get(pb0 - 1))
            waitp("vector", out_t.get(pb0 - 2))
            wait("vector", s_ve, b1_ev[(OC_OUT - 1, tc)])
            V(lambda e, t=tc, s=slot, o=oth: e.tensor_scalar(
                out=sc[:, o:o + 2048], in0=sc[:, s:s + 2048],
                scalar1=stv(C_M1 + t), scalar2=stv(C_R1 + t),
                op0=ALU.subtract, op1=ALU.mult))
            V(lambda e, t=tc, o=oth: e.scalar_tensor_tensor(
                out=big[:, t * 2048:t * 2048 + 2048], in0=sc[:, o:o + 2048],
                scalar=1.0, in1=big[:, t * 2048:t * 2048 + 2048],
                op0=ALU.mult, op1=ALU.add, accum_out=stv(C_T)))
            af_t = s_ve.n
            wait("scalar", s_ve, af_t)
            wait("scalar", s_ve, sp_guard[0])
            for j in range(2):
                SC(lambda e, t=tc, j_=j: e.activation(
                    out=ternc2[:, 1024:2048],
                    in_=big[:, t * 2048 + j_ * 1024:t * 2048 + (j_ + 1) * 1024],
                    func=AF.Square, accum_out=stv(56 + j_)))
            sq2_t = s_se.n
            wait("vector", s_se, sq2_t)
            V(lambda e: e.tensor_add(out=stv(C_T + 1), in0=stv(56), in1=stv(57)))
            sp_guard[0] = s_ve.n
            var_t = ln_stats_from(C_T, C_T + 1, EPS1, 1.0 / D)
            ln_sqrt_recip(var_t, EPS1)
            V(lambda e, t=tc, o=oth: e.tensor_scalar(
                out=sc[:, o:o + 2048], in0=big[:, t * 2048:t * 2048 + 2048],
                scalar1=stv(C_M), scalar2=stv(C_R), op0=ALU.subtract, op1=ALU.mult))
            V(lambda e, t=tc, s=slot, o=oth: e.tensor_add(
                out=big[:, t * 2048:t * 2048 + 2048], in0=sc[:, o:o + 2048],
                in1=sc[:, s:s + 2048]))
            p5_done[tc] = s_ve.n
            tern_step(4, "both")

        wait("sync", s_cc, cc2_t)
        go_t = dma("sync", SL, lambda e: e.dma_start(
            out=trow[0:2, 132:133],
            in_=gam_out.ap().rearrange("(p j) -> p j", j=1)))
        waitp("vector", go_t)
        V(lambda e: e.tensor_scalar(out=trow[0:2, 133:134],
                                    in0=trow[0:2, 132:133],
                                    scalar1=0.5 / WCNT, scalar2=None,
                                    op0=ALU.mult))
        V(lambda e: e.tensor_scalar(out=trow[0:2, 134:135],
                                    in0=trow[0:2, 133:134],
                                    scalar1=0.5 * EPS2, scalar2=None,
                                    op0=ALU.add))
        V(lambda e: e.tensor_scalar(out=trow[0:2, 135:136],
                                    in0=trow[0:2, 134:135],
                                    scalar1=-1.0, scalar2=None, op0=ALU.mult))
        g6_t = s_ve.n
        wait("sync", s_ve, g6_t)
        t6a = dma("sync", SL, lambda e: e.dma_start(
            out=tern6.ap().rearrange("(p j) -> p j", j=3),
            in_=trow[0:2, 133:136]))
        waitp("sync", t6a)
        ternbc_rec[0] = dma("sync", SL, lambda e: e.dma_start(
            out=ternbc[:, 0:3],
            in_=tern6[0:3][None, :].broadcast_to([128, 3])))
        ternbc_rec[1] = dma("sync", SL, lambda e: e.dma_start(
            out=ternbc[:, 3:6],
            in_=tern6[3:6][None, :].broadcast_to([128, 3])))

        fc_grp, fc_ev = {}, {}
        pj_grp, pj_ev = {}, {}
        out_t = {}
        ring_rel = {}
        pj_ld_ord = [0]
        gidx_tr = [0]

        def harena(ti):
            a = arnE if ti < 2 else arnF
            return a, (ti % 2) * 8192

        for bi in range(NBLK):
            for tc in range(bi * BT, (bi + 1) * BT):
                emit_p5(tc)
            t2_4s[0] = False
            for ti in range(BT):
                tc = bi * BT + ti
                wait("scalar", s_ve, p5_done[tc])
                waitp("vector", t2_last_use.get("sA"))
                waitp("vector", t2_last_use.get("sB"))
                wait("vector", s_se, t2_sign[0])
                if ti == 0 and bi > 0:
                    waitp("vector", out_t.get(bi * BT - 1))
                    waitp("vector", out_t.get(bi * BT - 2))
                stats_pair(big, tc * 2048)
                var_t = ln_stats_from(C_T, C_T + 1, EPS1, 1.0 / D)
                ln_sqrt_recip(var_t, EPS1)
                V(lambda e, t=tc: e.tensor_scalar(
                    out=sc[:, 0:2048], in0=big[:, t * 2048:t * 2048 + 2048],
                    scalar1=stv(C_M), scalar2=stv(C_R),
                    op0=ALU.subtract, op1=ALU.mult))
                xn2_t = s_ve.n
                wait("scalar", s_ve, xn2_t)
                stats_pair(sc, 0)
                var_t = ln_stats_from(C_T, C_T + 1, EPS2, 1.0 / D)
                ln_sqrt_recip(var_t, EPS2)
                V(lambda e: e.tensor_scalar(
                    out=sc[:, 2048:4096], in0=sc[:, 0:2048], scalar1=stv(C_M),
                    scalar2=stv(C_R), op0=ALU.subtract, op1=ALU.mult))
                V(lambda e: e.reduce_max(out=stv(C_G), in_=sc[:, 2048:4096], axis=AX.X,
                                         apply_absolute_value=True))
                V(lambda e: e.tensor_scalar(out=stv(C_T + 6), in0=stv(C_G),
                                            scalar1=CLIPC, scalar2=None, op0=ALU.mult))
                V(lambda e: e.tensor_sub(out=stv(C_CL), in0=stv(C_G), in1=stv(C_T + 6)))
                V(lambda e: e.tensor_scalar(out=stv(C_NCL), in0=stv(C_CL),
                                            scalar1=-1.0, scalar2=None, op0=ALU.mult))
                V(lambda e: e.tensor_scalar(
                    out=scb[:, 0:2048], in0=sc[:, 2048:4096], scalar1=stv(C_CL),
                    scalar2=stv(C_NCL), op0=ALU.min, op1=ALU.max))
                xh_t = s_ve.n
                wait("tensor", s_ve, xh_t)
                for g in range(2):
                    tr_group(
                        (lambda j, g_=g: scb[:, (g_ * 8 + j) * 128:
                                             (g_ * 8 + j + 1) * 128]),
                        arnA[:, ti * 2048 + g * 1024: ti * 2048 + (g + 1) * 1024])
                tern_step(6, "t2")
            XH_ALL = s_ve.n

            t2_4s[0] = True
            for oc in range(OC_FC):
                if oc >= 2:
                    rw = (s_mm, fc_grp[(bi, oc - 2, BT - 1)])
                elif bi > 0:
                    rw = (s_mm, pj_grp[(bi - 1, 3, BT - 1)])
                else:
                    rw = (s_mm, oc_last[OC_QKV + OC_OUT - 1])
                ncut = 16 * (oc // 2 + 1)
                while st_rec.get(ncut - 1) is None and t2_done[0] < N_T2:
                    t2_step(4)
                for z in range(1, 5):
                    waitp("sync", st_rec.get(ncut - z))
                    waitp("scalar", st_rec.get(ncut - z))
                wait("sync", rw[0], rw[1])
                wait("scalar", rw[0], rw[1])
                def ldf(e, o=oc):
                    return e.dma_start(
                        out=arnB[:, (o % 2) * 8192:(o % 2) * 8192 + 4096],
                        in_=wfc_t[o, :, 0:4096])
                def ldf2(e, o=oc):
                    return e.dma_start(
                        out=arnB[:, (o % 2) * 8192 + 4096:(o % 2) * 8192 + 8192],
                        in_=wfc_t[o, :, 4096:8192])
                fcs_t = dma("sync", SG, ldf)
                fcs2_t = dma("scalar", SG2, ldf2)

                for ti in range(BT):
                    b = grp[0] % 5
                    waitp("tensor", fcs_t)
                    waitp("tensor", fcs2_t)
                    if oc == 0 and ti == 0:
                        wait("tensor", s_ve, XH_ALL)
                    bank_wait(b)
                    for k in range(KD):
                        def mmf(e, t=ti, k_=k, b_=b, o=oc, last=(k == KD - 1)):
                            inst = e.matmul(
                                pb[b_][:, 0:512],
                                arnA[:, (t * KD + k_) * 128:(t * KD + k_ + 1) * 128],
                                slab_ap(o % 2, k_), start=(k_ == 0), stop=last)
                            if last:
                                inst.then_inc(s_mm.h, 1)
                        T(mmf)
                    fc_grp[(bi, oc, ti)] = s_mm.bump(1)
                    grp[0] += 1
                    wait("scalar", s_mm, fc_grp[(bi, oc, ti)])
                    if bi == 0 and oc == 0 and ti == 0:
                        waitp("scalar", ternbc_rec[0])
                    ha, hoff = harena(ti)
                    def evf(e, b_=b, a=ha, o=hoff + oc * 512, c=oc * BT + ti):
                        return e.activation(out=a[:, o:o + 512], in_=pb[b_][:, 0:512],
                                            func=AF.Gelu, scale=ternbc[:, 0:1],
                                            accum_out=hsum[:, c:c + 1])
                    SC(evf)
                    fc_ev[(bi, oc, ti)] = s_se.n
                    bank_free[b] = (s_se, fc_ev[(bi, oc, ti)])
                    tern_step(2, "t2")

            t2_4s[0] = False
            for ti in range(BT):
                ha, hoff = harena(ti)
                wait("vector", s_se, fc_ev[(bi, OC_FC - 1, ti)])
                waitp("vector", t2_last_use.get("sA"))
                waitp("vector", t2_last_use.get("sB"))
                wait("vector", s_se, t2_sign[0])
                V(lambda e, t=ti: e.reduce_sum(
                    out=stv(C_T),
                    in_=hsum[:, 0:OC_FC * BT].rearrange("p (o t) -> p o t", t=BT)[:, :, t],
                    axis=AX.X))
                wait("scalar", s_ve, sp_guard[0])
                for j in range(8):
                    SC(lambda e, a=ha, o=hoff + j * 1024, j_=j: e.activation(
                        out=ternc2[:, 1024:2048], in_=a[:, o:o + 1024],
                        func=AF.Square, accum_out=stv(52 + j_)))
                sqh_t = s_se.n
                wait("vector", s_se, sqh_t)
                V(lambda e: e.tensor_add(out=stv(52), in0=stv(52), in1=stv(53)))
                V(lambda e: e.tensor_add(out=stv(54), in0=stv(54), in1=stv(55)))
                V(lambda e: e.tensor_add(out=stv(56), in0=stv(56), in1=stv(57)))
                V(lambda e: e.tensor_add(out=stv(58), in0=stv(58), in1=stv(59)))
                V(lambda e: e.tensor_add(out=stv(52), in0=stv(52), in1=stv(54)))
                V(lambda e: e.tensor_add(out=stv(56), in0=stv(56), in1=stv(58)))
                V(lambda e: e.tensor_add(out=stv(C_T + 1), in0=stv(52),
                                         in1=stv(56)))
                sp_guard[0] = s_ve.n
                var_t = ln_stats_from(C_T, C_T + 1, EPS2, 1.0 / DFF)
                ln_sqrt_recip(var_t, EPS2)
                V(lambda e, a=ha, o=hoff: e.tensor_scalar(
                    out=scb[:, 0:8192], in0=a[:, o:o + 8192], scalar1=stv(C_M),
                    scalar2=None, op0=ALU.subtract, op1=ALU.max,
                    accum_out=stv(C_T + 6)))
                V(lambda e, a=ha, o=hoff: e.tensor_scalar(
                    out=scb[:, 0:8192], in0=a[:, o:o + 8192], scalar1=stv(C_M),
                    scalar2=None, op0=ALU.subtract, op1=ALU.min,
                    accum_out=stv(C_T + 7)))
                V(lambda e: e.tensor_scalar(out=stv(C_T + 7), in0=stv(C_T + 7),
                                            scalar1=-1.0, scalar2=None, op0=ALU.mult))
                V(lambda e: e.tensor_max(out=stv(C_T + 6), in0=stv(C_T + 6),
                                         in1=stv(C_T + 7)))
                V(lambda e: e.tensor_tensor(out=stv(C_G), in0=stv(C_T + 6),
                                            in1=stv(C_R), op=ALU.mult))
                V(lambda e: e.tensor_scalar(out=stv(C_T + 7), in0=stv(C_G),
                                            scalar1=CLIPC, scalar2=None, op0=ALU.mult))
                V(lambda e: e.tensor_sub(out=stv(C_CL), in0=stv(C_G), in1=stv(C_T + 7)))
                V(lambda e: e.tensor_scalar(out=stv(C_NCL), in0=stv(C_CL),
                                            scalar1=-1.0, scalar2=None, op0=ALU.mult))
                for q in range(4):
                    V(lambda e, a=ha, o=hoff + q * 2048: e.tensor_scalar(
                        out=scb[:, 0:2048], in0=a[:, o:o + 2048],
                        scalar1=stv(C_M), scalar2=stv(C_R),
                        op0=ALU.subtract, op1=ALU.mult))
                    def hh2(e, a=ha, o=hoff + q * 2048):
                        return e.tensor_scalar(
                            out=a[:, o:o + 2048], in0=scb[:, 0:2048],
                            scalar1=stv(C_CL), scalar2=stv(C_NCL),
                            op0=ALU.min, op1=ALU.max)
                    V(hh2)
                hh_t = s_ve.n
                t2_step(2)
                wait("tensor", s_ve, hh_t)
                for g in range(8):
                    tr_group(
                        (lambda j, a=ha, o=hoff, g_=g: a[:, o + (g_ * 8 + j) * 128:
                                                        o + (g_ * 8 + j + 1) * 128]),
                        ha[:, hoff + g * 1024: hoff + (g + 1) * 1024])
            HH_ALL = s_ve.n

            t2_4s[0] = True
            for ech in range(4):
                for j4 in range(KFF // 4):
                    g_ord = pj_ld_ord[0]
                    pj_ld_ord[0] += 1
                    if g_ord - bi * 64 < 4:
                        wait("sync", s_mm, fc_grp[(bi, OC_FC - 2, BT - 1)])
                    else:
                        rr = ring_rel[g_ord - 4]
                        wait("sync", rr[0], rr[1])
                    ncut = 128 + 64 * (ech // 2 + 1)
                    while st_rec.get(ncut - 1) is None and t2_done[0] < N_T2:
                        t2_step(4)
                    for z in range(1, 5):
                        waitp("sync", st_rec.get(ncut - z))
                    def ldp(e, e_=ech, j_=j4):
                        return e.dma_start(
                            out=arnB[:, (j_ % 4) * 2048:(j_ % 4) * 2048 + 2048]
                            .rearrange("p (g o) -> p g o", g=4),
                            in_=wproj_t[e_, j_ * 4:(j_ + 1) * 4, :, :]
                            .rearrange("g p o -> p g o"))
                    pjld_t = dma("sync", SG, ldp)
                    tern_step(2, "t2")
                    for jj in range(4):
                        j = j4 * 4 + jj
                        if ech == 0 and jj == 0 and j4 % 2 == 0:
                            wait("tensor", s_ve, HH_ALL)
                        if jj == 0:
                            waitp("tensor", pjld_t)
                        if j == 0:
                            for ti in range(BT):
                                bank_wait(ti)
                        for ti in range(BT):
                            def mmp(e, t=ti, j_=j, last_g=(j == KFF - 1),
                                    ring=(jj == 3 and ti == BT - 1 and j != KFF - 1)):
                                a, o = harena(t)
                                inst = e.matmul(
                                    pb[t][:, 0:512],
                                    a[:, o + j_ * 128:o + (j_ + 1) * 128],
                                    arnB[:, (j_ % 16) * 512:(j_ % 16 + 1) * 512],
                                    start=(j_ == 0), stop=last_g)
                                if last_g:
                                    inst.then_inc(s_mm.h, 1)
                                elif ring:
                                    inst.then_inc(s_mp.h, 1)
                            T(mmp)
                            if j == KFF - 1:
                                pj_grp[(bi, ech, ti)] = s_mm.bump(1)
                    if j4 == KFF // 4 - 1:
                        ring_rel[g_ord] = (s_mm, s_mm.n)
                    else:
                        ring_rel[g_ord] = (s_mp, s_mp.bump(1))
                for ti in range(BT):
                    wait("scalar", s_mm, pj_grp[(bi, ech, ti)])
                    if bi == 0 and ech == 0 and ti == 0:
                        waitp("scalar", ternbc_rec[1])
                    def evp(e, t=ti, e_=ech):
                        return e.activation(out=arnA[:, 8192 + t * 2048 + e_ * 512:
                                                     8192 + t * 2048 + (e_ + 1) * 512],
                                            in_=pb[t][:, 0:512], func=AF.Copy,
                                            scale=ternbc[:, 3:4])
                    SC(evp)
                    pj_ev[(bi, ech, ti)] = s_se.n
                    bank_free[ti] = (s_se, pj_ev[(bi, ech, ti)])

            t2_4s[0] = False
            for ti in range(BT):
                tc = bi * BT + ti
                slot = (tc % 2) * 2048
                oth = 2048 - slot
                waitp("vector", out_t.get(tc - 1))
                waitp("vector", t2_last_use.get("sA"))
                waitp("vector", t2_last_use.get("sB"))
                wait("vector", s_se, t2_sign[0])
                wait("scalar", s_se, pj_ev[(bi, 3, ti)])
                stats_pair(arnA, 8192 + ti * 2048)
                var_t = ln_stats_from(C_T, C_T + 1, EPS1, 1.0 / D)
                ln_sqrt_recip(var_t, EPS1)
                V(lambda e, t=ti, o=oth: e.tensor_scalar(
                    out=sc[:, o:o + 2048],
                    in0=arnA[:, 8192 + t * 2048:8192 + t * 2048 + 2048],
                    scalar1=stv(C_M), scalar2=stv(C_R),
                    op0=ALU.subtract, op1=ALU.mult))
                V(lambda e, t=tc, s=slot, o=oth: e.tensor_add(
                    out=sc[:, s:s + 2048], in0=sc[:, o:o + 2048],
                    in1=big[:, t * 2048:t * 2048 + 2048]))
                x3_t = s_ve.n
                wait("sync", s_ve, x3_t)
                out_t[tc] = dma("sync", OU, lambda e, t=tc, s=slot: e.dma_start(
                    out=out[t * 128:(t + 1) * 128, :], in_=sc[:, s:s + 2048]))
                tern_step(2, "t2")

        for semx in OU.sems:
            wait("sync", semx, semx.n)

        @blk.sync
        def _(e):
            for f in OPS["sync"]:
                f(e)

        @blk.gpsimd
        def _(e):
            for f in OPS["gpsimd"]:
                f(e)

        @blk.tensor
        def _(e):
            for f in OPS["tensor"]:
                f(e)

        @blk.vector
        def _(e):
            for f in OPS["vector"]:
                f(e)

        @blk.scalar
        def _(e):
            for f in OPS["scalar"]:
                f(e)

    return nc



def _host_layout(W_in, b_in, W_out, w_fc, w_proj):
    perm = np.concatenate([np.arange(0, 1024, 2), np.arange(1, 1024, 2),
                           np.arange(1024, 2048)])
    order = np.concatenate([4096 + perm, 2048 + perm, 0 + perm])
    winT = np.ascontiguousarray(np.asarray(W_in, dtype=np.float32)[order, :].T)
    binb = np.ascontiguousarray(np.asarray(b_in, dtype=np.float32)[order])
    woutT = np.ascontiguousarray(np.asarray(W_out, dtype=np.float32)[:, perm].T)
    wfcT = np.ascontiguousarray(np.asarray(w_fc, dtype=np.float32).T)
    wprojT = np.ascontiguousarray(np.asarray(w_proj, dtype=np.float32).T)
    return winT, binb, woutT, wfcT, wprojT


def _p1_shard(wfcT, wprojT, c):
    chs = []
    for i in range(16 * c, 16 * c + 16):
        kp, ocq = divmod(i, 8)
        chs.append(wfcT[kp * 128:(kp + 1) * 128, ocq * 1024:(ocq + 1) * 1024])
    for i in range(16 * c, 16 * c + 16):
        r, cq = divmod(i, 2)
        chs.append(wprojT[r * 128:(r + 1) * 128, cq * 1024:(cq + 1) * 1024])
    return np.ascontiguousarray(np.concatenate(chs, axis=0))


def _tables(positions):
    inv = 10000.0 ** (-np.arange(512, dtype=np.float64) / 512.0)
    ang = positions[:, None].astype(np.float64) * inv[None, :]
    c64, s64 = np.cos(ang), np.sin(ang)
    ch = c64.astype(np.float32)
    cl = (c64 - ch.astype(np.float64)).astype(np.float32)
    sh = s64.astype(np.float32)
    sl = (s64 - sh.astype(np.float64)).astype(np.float32)
    return np.ascontiguousarray(
        np.concatenate([ch, cl, sh, sl], axis=1).astype(np.float32))


def kernel(L0_attn_X, L0_attn_W_in, L0_attn_b_in, L0_attn_W_out, L0_attn_b_out,
           L0_ffn_w_fc, L0_ffn_w_proj):
    X = np.ascontiguousarray(np.asarray(L0_attn_X), dtype=np.float32)
    B, S, _ = X.shape
    n_cores = 8
    cpb = n_cores // B
    NT = S // cpb
    n_tc = NT // 128
    winT, binb, woutT, wfcT, wprojT = _host_layout(
        L0_attn_W_in, L0_attn_b_in, L0_attn_W_out, L0_ffn_w_fc, L0_ffn_w_proj)
    boutb = np.ascontiguousarray(np.asarray(L0_attn_b_out, dtype=np.float32))
    identf = np.eye(128, dtype=np.float32)

    if n_tc not in _CACHE:
        _CACHE[n_tc] = build(n_tc)
    nc = _CACHE[n_tc]

    in_maps = []
    for c in range(n_cores):
        b = c // cpb
        s0 = (c % cpb) * NT
        in_maps.append(dict(
            x=np.ascontiguousarray(X[b, s0:s0 + NT, :]),
            winT=winT, binb=binb, woutT=woutT, boutb=boutb,
            wfcT=wfcT, wprojT=wprojT,
            wsh=_p1_shard(wfcT, wprojT, c),
            tab=_tables(np.arange(s0, s0 + NT)),
            identf=identf,
        ))
    res = run_bass_kernel_spmd(nc, in_maps, core_ids=list(range(n_cores)))
    shards = [np.asarray(res.results[c]["out"]) for c in range(n_cores)]
    full = np.concatenate(shards, axis=0).reshape(B, S, D)
    return full


# version 18
# speedup vs baseline: 1.0493x; 1.0493x over previous
import contextlib

import numpy as np
import ml_dtypes

import concourse.bass as bass
import concourse.mybir as mybir
from concourse.bass_utils import run_bass_kernel_spmd

F32 = mybir.dt.float32
BF16 = mybir.dt.bfloat16
AF = mybir.ActivationFunctionType
ALU = mybir.AluOpType
AX = mybir.AxisListType

D = 2048
DFF = 8192
KD = D // 128
KFF = DFF // 128
OC_QKV = 12
OC_OUT = 4
OC_FC = 16
EPS1 = 1e-5
EPS2 = 1e-6
CLIPC = 7.8125e-9
WCNT = float(D * DFF)
N_P1 = 32
T2_SIGN = False

_CACHE = {}


class Sem:
    def __init__(self, h):
        self.h = h
        self.n = 0

    def bump(self, k=1):
        self.n += k
        return self.n


class DmaS:

    def __init__(self, sems):
        self.sems = sems
        self.i = 0

    def next(self):
        sem = self.sems[self.i % len(self.sems)]
        self.i += 1
        pre = sem.n
        post = sem.bump(16)
        return sem, pre, post


_BT_CAP = 4


def build(n_tc):
    NT = n_tc * 128
    BT = min(_BT_CAP, n_tc)
    NBLK = n_tc // BT
    nc = bass.Bass("TRN2", target_bir_lowering=False)

    x_in = nc.dram_tensor("x", [NT, D], F32, kind="ExternalInput")
    winT = nc.dram_tensor("winT", [D, 3 * D], F32, kind="ExternalInput")
    binb = nc.dram_tensor("binb", [3 * D], F32, kind="ExternalInput")
    woutT = nc.dram_tensor("woutT", [D, D], F32, kind="ExternalInput")
    boutb = nc.dram_tensor("boutb", [D], F32, kind="ExternalInput")
    wfcT = nc.dram_tensor("wfcT", [D, DFF], F32, kind="ExternalInput")
    wprojT = nc.dram_tensor("wprojT", [DFF, D], F32, kind="ExternalInput")
    tab = nc.dram_tensor("tab", [NT, 2048], F32, kind="ExternalInput")
    identf = nc.dram_tensor("identf", [128, 128], F32, kind="ExternalInput")
    out = nc.dram_tensor("out", [NT, D], F32, kind="ExternalOutput")
    cc_in = nc.dram_tensor("cc_in", [D], F32)
    cc_out = nc.dram_tensor("cc_out", [D], F32)
    wfc_t = nc.dram_tensor("wfc_t", [OC_FC, 128, DFF], BF16)
    wproj_t = nc.dram_tensor("wproj_t", [4, KFF, 128, 512], BF16)
    tern6 = nc.dram_tensor("tern6", [6], F32)
    tpre = nc.dram_tensor("tpre", [256], F32)
    wsh = nc.dram_tensor("wsh", [N_P1 * 128, 1024], F32, kind="ExternalInput")
    gam_in = nc.dram_tensor("gam_in", [2], F32)
    gam_out = nc.dram_tensor("gam_out", [2], F32)

    OPS = {k: [] for k in ("tensor", "vector", "scalar", "sync", "gpsimd")}

    def T(f):
        OPS["tensor"].append(f)

    CH = {}

    def V(f):
        tgt = CH.get("v", 0)

        def g(e, f=f, t=tgt):
            if t:
                e.wait_ge(s_ve.h, t)
            f(e).then_inc(s_ve.h, 1)
        OPS["vector"].append(g)
        CH["v"] = s_ve.bump(1)
        return CH["v"]

    def SC(f):
        tgt = CH.get("s", 0)

        def g(e, f=f, t=tgt):
            if t:
                e.wait_ge(s_se.h, t)
            f(e).then_inc(s_se.h, 1)
        OPS["scalar"].append(g)
        CH["s"] = s_se.bump(1)
        return CH["s"]

    def GC(f):
        tgt = CH.get("g", 0)

        def g(e, f=f, t=tgt):
            if t:
                e.wait_ge(s_gp.h, t)
            f(e).then_inc(s_gp.h, 1)
        OPS["gpsimd"].append(g)
        CH["g"] = s_gp.bump(1)
        return CH["g"]

    def SY(f):
        OPS["sync"].append(f)

    def G(f):
        OPS["gpsimd"].append(f)

    def wait(q, sem, tgt):
        if tgt is None or tgt <= 0:
            return
        OPS[q].append(lambda e, h=sem.h, t=tgt: e.wait_ge(h, t))

    def waitp(q, rec):
        if rec:
            wait(q, rec[0], rec[1])

    def dma(q, stream, fn):
        sem, pre, post = stream.next()
        wait(q, sem, pre)
        OPS[q].append(lambda e, f=fn, h=sem.h: f(e).then_inc(h, 16))
        return (sem, post)

    with contextlib.ExitStack() as es:
        E = es.enter_context
        AR = 16384
        arnA = E(nc.sbuf_tensor("arnA", [128, AR], BF16))
        arnB = E(nc.sbuf_tensor("arnB", [128, AR], BF16))
        arnE = E(nc.sbuf_tensor("arnE", [128, AR], BF16))
        arnF = E(nc.sbuf_tensor("arnF", [128, AR], BF16))
        big = E(nc.sbuf_tensor("big", [128, AR], BF16))
        sc = E(nc.sbuf_tensor("sc", [128, 4096], F32))
        scb = sc.ap().bitcast(BF16)
        tabb = E(nc.sbuf_tensor("tabb", [128, 2048], F32))
        tabbb = tabb.ap().bitcast(BF16)
        ternw = E(nc.sbuf_tensor("ternw", [128, 2048], F32))
        ternt = E(nc.sbuf_tensor("ternt", [128, 2048], BF16))
        ternc2 = E(nc.sbuf_tensor("ternc2", [128, 2048], BF16))
        biasb = E(nc.sbuf_tensor("biasb", [128, 1024], F32))
        idf = E(nc.sbuf_tensor("idf", [128, 128], F32))
        idb = E(nc.sbuf_tensor("idb", [128, 128], BF16))
        onesb = E(nc.sbuf_tensor("onesb", [128, 8], BF16))
        st_ = E(nc.sbuf_tensor("stats", [128, 160], F32))
        asumP = E(nc.sbuf_tensor("asumP", [128, 256], F32))
        a_acc = E(nc.sbuf_tensor("a_acc", [128, 16], F32))
        arow = E(nc.sbuf_tensor("arow", [128, 128], F32))
        trow = E(nc.sbuf_tensor("trow", [128, 136], F32))
        ternbc = E(nc.sbuf_tensor("ternbc", [128, 8], F32))
        hsum = E(nc.sbuf_tensor("hsum", [128, 64], F32))

        pb = [E(nc.psum_tensor("pb%d" % i, [128, 512], F32)) for i in range(5)]
        pb5 = E(nc.psum_tensor("pb5", [128, 512], F32))
        pA = E(nc.psum_tensor("pA", [128, 512], F32))
        ptr = E(nc.psum_tensor("ptr", [128, 1024], BF16))
        ptr2 = pb5.ap().bitcast(BF16)

        SL = DmaS([Sem(E(nc.semaphore("sl%d" % i))) for i in range(2)])
        TW = DmaS([Sem(E(nc.semaphore("tw%d" % i))) for i in range(2)])
        TW2 = DmaS([Sem(E(nc.semaphore("tx%d" % i))) for i in range(4)])
        ST = DmaS([Sem(E(nc.semaphore("st%d" % i))) for i in range(4)])
        SG = DmaS([Sem(E(nc.semaphore("sg%d" % i))) for i in range(2)])
        SG2 = DmaS([Sem(E(nc.semaphore("sh%d" % i))) for i in range(2)])
        WL = DmaS([Sem(E(nc.semaphore("wl%d" % i))) for i in range(2)])
        OU = DmaS([Sem(E(nc.semaphore("ou%d" % i))) for i in range(2)])
        s_cc = Sem(E(nc.semaphore("s_cc")))
        s_gp = Sem(E(nc.semaphore("s_gp")))
        s_mm = Sem(E(nc.semaphore("s_mm")))
        s_mp = Sem(E(nc.semaphore("s_mp")))
        s_tp = Sem(E(nc.semaphore("s_tp")))
        s_ve = Sem(E(nc.semaphore("s_ve")))
        s_se = Sem(E(nc.semaphore("s_se")))

        blk = E(nc.Block())

        def stv(i):
            return st_[:, i:i + 1]

        C_M1, C_R1 = 0, 16
        C_T = 32
        C_KS, C_RINV = 40, 41
        C_M, C_R = 42, 43
        C_G, C_CL, C_NCL = 44, 45, 46
        C_S2 = 48

        bank_free = {}

        def bank_wait(b):
            t = bank_free.get(b)
            if t:
                wait("tensor", t[0], t[1])

        trp_free = {0: None, 1: None}
        tr_par = [0]

        def tr_group(src_ap_fn, out_ap):
            p = tr_par[0] % 2
            tr_par[0] += 1
            dst = ptr if p == 0 else ptr2
            t = trp_free[p]
            if t:
                wait("tensor", t[0], t[1])
            for j in range(8):
                def tr(e, j_=j, d=dst, fn=src_ap_fn, last=(j == 7)):
                    inst = e.transpose(d[:, j_ * 128:(j_ + 1) * 128], fn(j_), idb[:])
                    if last:
                        inst.then_inc(s_tp.h, 1)
                T(tr)
            tp_t = s_tp.bump(1)
            wait("vector", s_tp, tp_t)
            V(lambda e, d=dst, o=out_ap: e.tensor_copy(out=o, in_=d[:, 0:1024])
              )
            ev_t = s_ve.n
            trp_free[p] = (s_ve, ev_t)
            return tp_t, ev_t

        def ln_stats_from(sum_col, ssq_col, eps, inv_n):
            V(lambda e: e.tensor_scalar(out=stv(C_M), in0=stv(sum_col),
                                        scalar1=inv_n, scalar2=None, op0=ALU.mult))
            V(lambda e: e.tensor_scalar(out=stv(C_T + 2), in0=stv(ssq_col),
                                        scalar1=inv_n, scalar2=None, op0=ALU.mult))
            V(lambda e: e.tensor_tensor(out=stv(C_T + 3), in0=stv(C_M), in1=stv(C_M),
                                        op=ALU.mult))
            V(lambda e: e.tensor_sub(out=stv(C_T + 4), in0=stv(C_T + 2),
                                     in1=stv(C_T + 3)))
            return s_ve.n

        sp_guard = [0]

        def stats_pair(src, soff):
            wait("scalar", s_ve, sp_guard[0])
            for j in range(2):
                SC(lambda e, o=soff + j * 1024, j_=j, a=src: e.activation(
                    out=ternc2[:, 1024:2048], in_=a[:, o:o + 1024], func=AF.Copy,
                    accum_out=stv(52 + j_)))
            for j in range(2):
                SC(lambda e, o=soff + j * 1024, j_=j, a=src: e.activation(
                    out=ternc2[:, 1024:2048], in_=a[:, o:o + 1024], func=AF.Square,
                    accum_out=stv(56 + j_)))
            t = s_se.n
            wait("vector", s_se, t)
            V(lambda e: e.tensor_add(out=stv(C_T), in0=stv(52), in1=stv(53)))
            V(lambda e: e.tensor_add(out=stv(C_T + 1), in0=stv(56), in1=stv(57)))
            sp_guard[0] = s_ve.n

        def ln_sqrt_recip(var_t, eps):
            epsc = 150 if eps == EPS1 else 151
            wait("scalar", s_ve, var_t)
            SC(lambda e, b=epsc: e.activation(out=stv(C_T + 5), in_=stv(C_T + 4),
                                             func=AF.Sqrt, bias=stv(b)))
            sq_t = s_se.n
            wait("vector", s_se, sq_t)
            V(lambda e: e.reciprocal(out=stv(C_R), in_=stv(C_T + 5)))

        T_IDF = dma("sync", SL, lambda e: e.dma_start(out=idf[:], in_=identf[:, :]))
        V(lambda e: e.memset(onesb[:], 1.0))
        V(lambda e: e.memset(st_[:, 150:151], EPS1))
        V(lambda e: e.memset(st_[:, 151:152], EPS2))
        waitp("vector", T_IDF)
        V(lambda e: e.tensor_copy(out=idb[:], in_=idf[:]))
        T_IDB = s_ve.n

        p1_ld, p1_dn = [0], [0]
        t2_ld_n, t2_dn = [0], [0]
        p1_ldrec, t2_ldrec = {}, {}
        t1_acc_t = {}
        st_rec = {}
        t2_strec = {}
        ternbc_rec = {}
        gates = {}

        t2_4s = [False]
        t2_slot_n = [0, 0]
        t2_last_use = {}
        t2_scrx = [0]
        t2_sign = [0]

        def t2_slot_ap(name):
            if name == "t0":
                return ternt[:, 0:1024]
            if name == "t1":
                return ternt[:, 1024:2048]
            if name == "sA":
                return scb[:, 4096:5120]
            return scb[:, 5120:6144]

        def t2_sd(i):
            if i < 128:
                ocq, kp = divmod(i, 16)
                src = wfcT[kp * 128:(kp + 1) * 128, ocq * 1024:(ocq + 1) * 1024]
                dst = wfc_t[2 * ocq:2 * ocq + 2, :, kp * 512:(kp + 1) * 512] \
                    .rearrange("o p c -> p o c")
                return src, dst, 1
            cq, r = divmod(i - 128, 64)
            src = wprojT[r * 128:(r + 1) * 128, cq * 1024:(cq + 1) * 1024]
            dst = wproj_t[2 * cq:2 * cq + 2, r, :, :].rearrange("e p c -> p e c")
            return src, dst, 4

        def tern_step(nn, which="both"):
            if which in ("p1", "both"):
                for _ in range(min(nn, N_P1 - p1_dn[0])):
                    i = p1_ld[0]
                    if i < N_P1 and i - p1_dn[0] < 2:
                        p1_ldrec[i] = dma(
                            "scalar", TW,
                            lambda e, ss=wsh[i * 128:(i + 1) * 128, :],
                            h=(i % 2) * 1024:
                            e.dma_start(out=ternw[:, h:h + 1024], in_=ss))
                        p1_ld[0] += 1
                    if p1_dn[0] < p1_ld[0]:
                        i = p1_dn[0]
                        waitp("scalar", p1_ldrec[i])
                        t1_acc_t[i] = SC(
                            lambda e, h=(i % 2) * 1024, i_=i: e.activation(
                                out=ternc2[:, 1024:2048], in_=ternw[:, h:h + 1024],
                                func=AF.Abs, accum_out=asumP[:, i_:i_ + 1]))
                        p1_dn[0] += 1
            if which in ("t2", "both"):
                for _ in range(min(nn, 256 - t2_dn[0])):
                    i = t2_ld_n[0]
                    if i < 256 and i - t2_dn[0] < 4:
                        if ((i // 128) not in ternbc_rec or "yt" not in gates
                                or p1_dn[0] < N_P1):
                            break
                        q2 = "sync" if i % 2 == 0 else "scalar"
                        if i < 4:
                            wait(q2, s_ve, gates["yt"])
                            wait(q2, s_se, t1_acc_t[N_P1 - 1])
                        if i >= 4:
                            wait(q2, s_ve, gates[("t2", i - 4)])
                        sl4 = i % 4
                        buf = ternw if sl4 < 2 else tabb
                        t2_ldrec[i] = dma(
                            q2, TW2,
                            lambda e, ss=t2_sd(i)[0], b=buf, h=(i % 2) * 1024:
                            e.dma_start(out=b[:, h:h + 1024], in_=ss))
                        t2_ld_n[0] += 1
                    if t2_dn[0] < t2_ld_n[0]:
                        i = t2_dn[0]
                        if i == 0:
                            waitp("vector", ternbc_rec[0])
                        if i == 128:
                            waitp("vector", ternbc_rec[1])
                        waitp("vector", t2_ldrec[i])
                        if t2_4s[0]:
                            sl = ("t0", "t1", "sA", "sB")[t2_slot_n[0] % 4]
                            t2_slot_n[0] += 1
                        else:
                            sl = ("t0", "t1")[t2_slot_n[1] % 2]
                            t2_slot_n[1] += 1
                        waitp("vector", t2_last_use.get(sl))
                        oap = t2_slot_ap(sl)
                        _, dst, thc = t2_sd(i)
                        buf = ternw if (i % 4) < 2 else tabb
                        h = (i % 2) * 1024
                        wap = buf[:, h:h + 1024]
                        if t2_4s[0] and i % 2 == 1 and T2_SIGN:
                            if t2_scrx[0]:
                                wait("scalar", s_ve, t2_scrx[0])
                            waitp("scalar", t2_ldrec[i])
                            waitp("scalar", ternbc_rec[i // 128])
                            waitp("scalar", t2_last_use.get(sl))
                            SC(lambda e, w=wap, c=thc: e.activation(
                                out=scb[:, 6144:7168], in_=w, func=AF.Sign,
                                bias=ternbc[:, c + 1:c + 2]))
                            SC(lambda e, w=wap, c=thc, o=oap: e.activation(
                                out=o, in_=w, func=AF.Sign,
                                bias=ternbc[:, c:c + 1]))
                            sa = s_se.n
                            t2_sign[0] = sa
                            wait("vector", s_se, sa)
                            gt = V(lambda e, o=oap: e.tensor_add(
                                out=o, in0=o, in1=scb[:, 6144:7168]))
                            t2_scrx[0] = gt
                        else:
                            V(lambda e, w=wap, c=thc: e.tensor_scalar(
                                out=ternc2[:, 0:1024], in0=w,
                                scalar1=ternbc[:, c + 1:c + 2], scalar2=2.0,
                                op0=ALU.is_lt, op1=ALU.mult))
                            V(lambda e, w=wap, c=thc, o=oap: e.tensor_scalar(
                                out=o, in0=w,
                                scalar1=ternbc[:, c:c + 1], scalar2=2.0,
                                op0=ALU.is_gt, op1=ALU.mult))
                            gt = V(lambda e, o=oap: e.tensor_sub(
                                out=o, in0=o, in1=ternc2[:, 0:1024]))
                        gates[("t2", i)] = gt
                        wait("gpsimd", s_ve, gt)
                        rec = dma("gpsimd", ST, lambda e, o=oap, d=dst:
                                  e.dma_start(out=d, in_=o))
                        st_rec[i] = rec
                        t2_strec[i] = rec
                        t2_last_use[sl] = rec
                        t2_dn[0] += 1

        def t1_step(nn):
            tern_step(nn, "p1")

        def t2_step(nn):
            tern_step(nn, "both")
        t2_done = t2_dn
        N_T2 = 256

        x_slot_free = {0: None, 1: None}
        for tc in range(n_tc):
            slot = (tc % 2) * 2048
            if x_slot_free[tc % 2]:
                wait("sync", s_ve, x_slot_free[tc % 2])
            x_rec = dma("sync", SL, lambda e, t=tc, s=slot: e.dma_start(
                out=sc[:, s:s + 2048], in_=x_in[t * 128:(t + 1) * 128, :]))
            waitp("scalar", x_rec)
            eoff = tc * 2048
            SC(lambda e, s=slot, d=eoff: e.activation(
                out=arnE[:, d:d + 2048], in_=sc[:, s:s + 2048], func=AF.Copy,
                accum_out=stv(C_T)))
            SC(lambda e, s=slot, d=eoff: e.activation(
                out=arnE[:, d:d + 2048], in_=sc[:, s:s + 2048], func=AF.Square,
                accum_out=stv(C_T + 1)))
            acc_t = s_se.n
            wait("vector", s_se, acc_t)
            var_t = ln_stats_from(C_T, C_T + 1, EPS1, 1.0 / D)
            ln_sqrt_recip(var_t, EPS1)
            V(lambda e, t=tc: e.tensor_copy(out=stv(C_M1 + t), in_=stv(C_M)))
            V(lambda e, t=tc: e.tensor_copy(out=stv(C_R1 + t), in_=stv(C_R)))
            V(lambda e, t=tc, s=slot, d=eoff: e.tensor_scalar(
                out=arnE[:, d:d + 2048], in0=sc[:, s:s + 2048],
                scalar1=stv(C_M), scalar2=stv(C_R),
                op0=ALU.subtract, op1=ALU.mult))
            u_t = s_ve.n
            x_slot_free[tc % 2] = u_t

            wait("tensor", s_ve, max(u_t, T_IDB))
            for g in range(2):
                tr_group(
                    (lambda j, t=tc, g_=g: arnE[:, t * 2048 + (g_ * 8 + j) * 128:
                                                t * 2048 + (g_ * 8 + j + 1) * 128]),
                    arnA[:, tc * 2048 + g * 1024: tc * 2048 + (g + 1) * 1024])
        ALL_XT = s_ve.n

        def slab_ap(slot, k):
            return arnB[:, slot * 8192 + k * 512: slot * 8192 + (k + 1) * 512]

        qkv_grp, qkv_ev = {}, {}
        oc_last = {}
        rope_done, fmkv_done, a_grp = {}, {}, {}
        tabload_t, bias_t, slab_t = {}, {}, {}
        grp = [0]

        def emit_slab(oc, reuse_wait):
            if reuse_wait:
                wait("gpsimd", reuse_wait[0], reuse_wait[1])
            if oc < OC_QKV:
                src = winT[:, oc * 512:(oc + 1) * 512]
            else:
                o = oc - OC_QKV
                src = woutT[:, o * 512:(o + 1) * 512]
            src = src.rearrange("(k p) o -> p k o", p=128)
            dst = arnB[:, (oc % 2) * 8192:(oc % 2) * 8192 + 8192].rearrange(
                "p (k o) -> p k o", k=KD)
            slab_t[oc] = dma("gpsimd", WL,
                             lambda e, s=src, d=dst: e.dma_start(out=d, in_=s))

        def emit_bias(oc, reuse_wait):
            if reuse_wait:
                wait("sync", reuse_wait[0], reuse_wait[1])
            src_t = binb if oc < OC_QKV else boutb
            o = oc if oc < OC_QKV else oc - OC_QKV
            bias_t[oc] = dma("sync", SL, lambda e, s=src_t, o_=o,
                             sl=(oc % 2) * 512: e.dma_start(
                out=biasb[:, sl:sl + 512],
                in_=s[o_ * 512:(o_ + 1) * 512][None, :].broadcast_to([128, 512])))

        def emit_table(r, tc, kind):
            if kind == "k":
                if r >= 2:
                    wait("sync", s_ve, rope_done[r - 2])
                par = (tc % 2) * 1024
                tabload_t[r] = dma("sync", SL, lambda e, t=tc, p=par: e.dma_start(
                    out=tabb[:, p:p + 1024].rearrange("p (f c) -> p f c", f=2),
                    in_=tab[t * 128:(t + 1) * 128, :]
                    .rearrange("p (f c) -> p f c", f=4)[:, 0::2, :]))
            else:
                if r >= 1:
                    wait("sync", s_ve, rope_done[r - 1])
                tabload_t[r] = dma("sync", SL, lambda e, t=tc: e.dma_start(
                    out=tabb[:], in_=tab[t * 128:(t + 1) * 128, :]))

        def rope14(base_arn, tcoff, table_t):
            r = [sc[:, i * 512:(i + 1) * 512] for i in range(6)]
            qe = base_arn[:, tcoff:tcoff + 512]
            qo = base_arn[:, tcoff + 512:tcoff + 1024]
            ch, cl = tabb[:, 0:512], tabb[:, 512:1024]
            sh, sl_ = tabb[:, 1024:1536], tabb[:, 1536:2048]
            waitp("vector", table_t)
            V(lambda e: e.tensor_mul(out=r[0], in0=qe, in1=sh))
            V(lambda e: e.tensor_mul(out=r[1], in0=qe, in1=sl_))
            V(lambda e: e.tensor_mul(out=r[2], in0=qe, in1=ch))
            V(lambda e: e.tensor_mul(out=r[3], in0=qe, in1=cl))
            V(lambda e: e.tensor_mul(out=r[4], in0=qo, in1=sh))
            V(lambda e: e.tensor_sub(out=r[5], in0=r[2], in1=r[4]))
            V(lambda e: e.tensor_mul(out=r[4], in0=qo, in1=sl_))
            V(lambda e: e.tensor_add(out=r[2], in0=r[5], in1=r[3]))
            V(lambda e: e.tensor_sub(out=qe, in0=r[2], in1=r[4]))
            V(lambda e: e.tensor_mul(out=r[3], in0=qo, in1=ch))
            V(lambda e: e.tensor_mul(out=r[5], in0=qo, in1=cl))
            V(lambda e: e.tensor_add(out=r[2], in0=r[3], in1=r[0]))
            V(lambda e: e.tensor_add(out=r[4], in0=r[5], in1=r[1]))
            V(lambda e: e.tensor_add(out=qo, in0=r[2], in1=r[4]))
            return s_ve.n

        emit_slab(0, None)
        emit_slab(1, None)
        emit_bias(0, None)
        emit_bias(1, None)

        def xT_ap(k, tc):
            off = (tc * KD + k) * 128
            return arnA[:, off:off + 128]

        for oc in range(OC_QKV):
            for tc in range(n_tc):
                b = grp[0] % 5
                waitp("tensor", slab_t[oc])
                if oc == 0 and tc == 0:
                    wait("tensor", s_ve, ALL_XT)
                bank_wait(b)
                for k in range(KD):
                    def mm(e, o=oc, t=tc, k_=k, b_=b, last=(k == KD - 1)):
                        inst = e.matmul(pb[b_][:, 0:512], xT_ap(k_, t),
                                        slab_ap(o % 2, k_), start=(k_ == 0), stop=last)
                        if last:
                            inst.then_inc(s_mm.h, 1)
                    T(mm)
                qkv_grp[(oc, tc)] = s_mm.bump(1)
                grp[0] += 1

                wait("vector", s_mm, qkv_grp[(oc, tc)])
                waitp("vector", bias_t[oc])
                if oc < 4:
                    dst, doff = arnE, tc * 2048 + oc * 512
                elif oc < 8:
                    dst, doff = arnF, tc * 2048 + (oc - 4) * 512
                else:
                    dst, doff = big, tc * 2048 + (oc - 8) * 512
                def evac(e, b_=b, d=dst, o_=doff, ob=(oc % 2) * 512):
                    return e.tensor_tensor(out=d[:, o_:o_ + 512], in0=pb[b_][:, 0:512],
                                           in1=biasb[:, ob:ob + 512], op=ALU.add)
                V(evac)
                qkv_ev[(oc, tc)] = s_ve.n
                bank_free[b] = (s_ve, qkv_ev[(oc, tc)])

                if oc == 7:
                    emit_table(tc, tc, "k")
                    par = (tc % 2) * 1024
                    r0 = sc[:, 0:512]
                    r1 = sc[:, 512:1024]
                    r2 = sc[:, 1024:1536]
                    ke = arnF[:, tc * 2048:tc * 2048 + 512]
                    ko = arnF[:, tc * 2048 + 512:tc * 2048 + 1024]
                    ch = tabb[:, par:par + 512]
                    sh = tabb[:, par + 512:par + 1024]
                    waitp("vector", tabload_t[tc])
                    V(lambda e, ke=ke, sh=sh: e.tensor_mul(out=r0, in0=ke, in1=sh))
                    V(lambda e, ke=ke, ch=ch: e.tensor_mul(out=r1, in0=ke, in1=ch))
                    V(lambda e, ko=ko, sh=sh: e.tensor_mul(out=r2, in0=ko, in1=sh))
                    V(lambda e, ke=ke: e.tensor_sub(out=ke, in0=r1, in1=r2))
                    V(lambda e, ko=ko, ch=ch: e.tensor_mul(out=r1, in0=ko, in1=ch))
                    rope_done[tc] = V(lambda e, ko=ko: e.tensor_add(out=ko, in0=r1, in1=r0))
                    V(lambda e, t=tc: e.scalar_tensor_tensor(
                        out=scb[:, 4096:6144],
                        in0=arnF[:, t * 2048:t * 2048 + 2048],
                        scalar=1.0, in1=arnF[:, t * 2048:t * 2048 + 2048],
                        op0=ALU.mult, op1=ALU.mult, accum_out=stv(C_KS)))
                    ks_t = s_ve.n
                    wait("scalar", s_ve, ks_t)
                    SC(lambda e: e.activation(out=stv(C_T + 6), in_=stv(C_KS),
                                              func=AF.Sqrt))
                    kn_t = s_se.n
                    wait("vector", s_se, kn_t)
                    V(lambda e: e.tensor_scalar(out=stv(C_T + 7), in0=stv(C_T + 6),
                                                scalar1=1e-12, scalar2=None,
                                                op0=ALU.max))
                    V(lambda e: e.reciprocal(out=stv(C_RINV), in_=stv(C_T + 7)))
                    for j in range(4):
                        def fm(e, t=tc, j_=j):
                            return e.scalar_tensor_tensor(
                                out=arnE[:, t * 2048 + j_ * 512:t * 2048 + (j_ + 1) * 512],
                                in0=arnF[:, t * 2048 + j_ * 512:t * 2048 + (j_ + 1) * 512],
                                scalar=stv(C_RINV),
                                in1=arnE[:, t * 2048 + j_ * 512:t * 2048 + (j_ + 1) * 512],
                                op0=ALU.mult, op1=ALU.mult)
                        V(fm)
                    fmkv_done[tc] = s_ve.n

                if oc == 8:
                    wait("tensor", s_ve, fmkv_done[tc])
                    for c in range(KD):
                        def am(e, c_=c, t=tc,
                               st=(tc == 0 and c == 0),
                               sp=(tc == n_tc - 1 and c == KD - 1),
                               last=(c == KD - 1)):
                            inst = e.matmul(pA[:, c_:c_ + 1],
                                            arnE[:, t * 2048 + c_ * 128:
                                                 t * 2048 + (c_ + 1) * 128],
                                            onesb[:, 0:1], start=st, stop=sp)
                            if last:
                                inst.then_inc(s_mm.h, 1)
                        T(am)
                    a_grp[tc] = s_mm.bump(1)

                if oc == 9:
                    emit_table(n_tc + tc, tc, "q")
                    rope_done[n_tc + tc] = rope14(big, tc * 2048, tabload_t[n_tc + tc])

                t1_step(3)

            oc_last[oc] = qkv_grp[(oc, n_tc - 1)]
            if oc + 2 < OC_QKV:
                emit_slab(oc + 2, (s_mm, oc_last[oc]))
                emit_bias(oc + 2, (s_ve, qkv_ev[(oc, n_tc - 1)]))
            if oc == 9:
                pass
            if oc == 10:
                emit_slab(12, (s_mm, oc_last[10]))
            if oc == 11:
                emit_slab(13, (s_mm, oc_last[11]))

            if oc == 8:
                wait("vector", s_mm, a_grp[n_tc - 1])
                V(lambda e: e.tensor_copy(out=a_acc[:], in_=pA[:, 0:16])
                  )
                aev_t = s_ve.n
                wait("tensor", s_ve, aev_t)
                T(lambda e: e.transpose(pA[0:16, 0:128], a_acc[:], idf[:])
                  .then_inc(s_tp.h, 1))
                atr_t = s_tp.bump(1)
                wait("vector", s_tp, atr_t)
                V(lambda e: e.tensor_copy(out=arow[0:16, 0:128], in_=pA[0:16, 0:128])
                  )
                arow_t = s_ve.n
                wait("sync", s_ve, arow_t)
                ccin_t = dma("sync", SL, lambda e: e.dma_start(
                    out=cc_in.ap().rearrange("(c p) -> c p", c=16),
                    in_=arow[0:16, 0:128]))

        waitp("gpsimd", ccin_t)
        G(lambda e: e.collective_compute(
            "AllReduce", ALU.add, replica_groups=[[0, 1, 2, 3], [4, 5, 6, 7]],
            ins=[cc_in[:]], outs=[cc_out[:]]).then_inc(s_cc.h, 1))
        cc_t = s_cc.bump(1)

        wait("vector", s_se, t1_acc_t[N_P1 - 1])
        V(lambda e: e.reduce_sum(out=stv(C_S2), in_=asumP[:, 0:16], axis=AX.X))
        V(lambda e: e.reduce_sum(out=stv(C_S2 + 1), in_=asumP[:, 16:32],
                                 axis=AX.X))
        gsum_t = s_ve.n
        wait("sync", s_ve, gsum_t)
        tp1 = dma("sync", SL, lambda e: e.dma_start(
            out=tpre.ap()[0:128].rearrange("(p j) -> p j", j=1),
            in_=st_[:, C_S2:C_S2 + 1]))
        waitp("sync", tp1)
        tp2 = dma("sync", SL, lambda e: e.dma_start(
            out=tpre.ap()[128:256].rearrange("(p j) -> p j", j=1),
            in_=st_[:, C_S2 + 1:C_S2 + 2]))
        waitp("sync", tp2)
        tp3 = dma("sync", SL, lambda e: e.dma_start(
            out=trow[0:2, 0:128],
            in_=tpre.ap().rearrange("(j p) -> j p", j=2)))
        waitp("vector", tp3)
        V(lambda e: e.reduce_sum(out=trow[0:2, 128:129], in_=trow[0:2, 0:128],
                                 axis=AX.X))
        gprt_t = s_ve.n
        wait("sync", s_ve, gprt_t)
        gi_t = dma("sync", SL, lambda e: e.dma_start(
            out=gam_in.ap().rearrange("(p j) -> p j", j=1),
            in_=trow[0:2, 128:129]))
        waitp("gpsimd", gi_t)
        G(lambda e: e.collective_compute(
            "AllReduce", ALU.add,
            replica_groups=[[0, 1, 2, 3, 4, 5, 6, 7]],
            ins=[gam_in[:]], outs=[gam_out[:]]).then_inc(s_cc.h, 1))
        cc2_t = s_cc.bump(1)
        wait("sync", s_cc, cc_t)
        wait("sync", s_ve, rope_done[2 * n_tc - 1])
        abc_t = dma("sync", SL, lambda e: e.dma_start(
            out=tabb[:], in_=cc_out[None, :].broadcast_to([128, 2048])))

        y_tp = {}
        for tc in range(n_tc):
            waitp("vector", abc_t)
            wait("vector", s_ve, qkv_ev[(11, tc)])
            if tc >= 2:
                wait("vector", s_tp, y_tp[tc - 2])
            ybuf = (tc % 2) * 2048
            for j in range(4):
                def ymul(e, t=tc, j_=j, yb=ybuf):
                    return e.tensor_mul(
                        out=scb[:, yb + j_ * 512:yb + (j_ + 1) * 512],
                        in0=tabb[:, j_ * 512:(j_ + 1) * 512],
                        in1=big[:, t * 2048 + j_ * 512:t * 2048 + (j_ + 1) * 512])
                V(ymul)
            y_t = s_ve.n
            wait("tensor", s_ve, y_t)
            for g in range(2):
                tp_t, _ = tr_group(
                    (lambda j, yb=ybuf, g_=g: scb[:, yb + (g_ * 8 + j) * 128:
                                                  yb + (g_ * 8 + j + 1) * 128]),
                    arnA[:, tc * 2048 + g * 1024: tc * 2048 + (g + 1) * 1024])
                if g == 1:
                    y_tp[tc] = tp_t
        ALL_YT = s_ve.n
        gates["yt"] = ALL_YT
        MM_B1 = s_mm.n
        emit_slab(14, (s_mm, MM_B1 + n_tc))
        emit_slab(15, (s_mm, MM_B1 + 2 * n_tc))

        emit_bias(OC_QKV + 0, (s_ve, qkv_ev[(10, n_tc - 1)]))
        emit_bias(OC_QKV + 1, (s_ve, qkv_ev[(11, n_tc - 1)]))
        b1_grp, b1_ev = {}, {}
        for oc in range(OC_OUT):
            for tc in range(n_tc):
                b = grp[0] % 5
                waitp("tensor", slab_t[OC_QKV + oc])
                if oc == 0 and tc == 0:
                    wait("tensor", s_ve, ALL_YT)
                bank_wait(b)
                for k in range(KD):
                    def mm2(e, t=tc, k_=k, b_=b, o=oc, last=(k == KD - 1)):
                        inst = e.matmul(pb[b_][:, 0:512], xT_ap(k_, t),
                                        slab_ap((OC_QKV + o) % 2, k_),
                                        start=(k_ == 0), stop=last)
                        if last:
                            inst.then_inc(s_mm.h, 1)
                    T(mm2)
                b1_grp[(oc, tc)] = s_mm.bump(1)
                grp[0] += 1
                wait("vector", s_mm, b1_grp[(oc, tc)])
                waitp("vector", bias_t[OC_QKV + oc])
                def evac2(e, b_=b, t=tc, o=oc, ob=(oc % 2) * 512):
                    return e.tensor_tensor(
                        out=big[:, t * 2048 + o * 512:t * 2048 + (o + 1) * 512],
                        in0=pb[b_][:, 0:512], in1=biasb[:, ob:ob + 512], op=ALU.add)
                V(evac2)
                b1_ev[(oc, tc)] = s_ve.n
                bank_free[b] = (s_ve, b1_ev[(oc, tc)])
                tern_step(3, "p1")
            oc_last[OC_QKV + oc] = b1_grp[(oc, n_tc - 1)]
            if oc == 0:
                emit_bias(OC_QKV + 2, (s_ve, b1_ev[(0, n_tc - 1)]))
            if oc == 1:
                emit_bias(OC_QKV + 3, (s_ve, b1_ev[(1, n_tc - 1)]))

        p5_done = {}

        def emit_p5(tc):
            slot = (tc % 2) * 2048
            oth = 2048 - slot
            if tc >= 1:
                wait("sync", s_ve, p5_done[tc - 1])
            else:
                wait("sync", s_ve, ALL_YT)
                wait("sync", s_tp, y_tp[n_tc - 1])
            pb0 = (tc // BT) * BT
            waitp("sync", out_t.get(pb0 - 1))
            waitp("sync", out_t.get(pb0 - 2))
            waitp("sync", t2_last_use.get("sA"))
            waitp("sync", t2_last_use.get("sB"))
            waitp("vector", t2_last_use.get("sA"))
            waitp("vector", t2_last_use.get("sB"))
            wait("sync", s_se, t2_sign[0])
            wait("vector", s_se, t2_sign[0])
            xre_t = dma("sync", SL, lambda e, t=tc, s=slot: e.dma_start(
                out=sc[:, s:s + 2048], in_=x_in[t * 128:(t + 1) * 128, :]))
            waitp("vector", xre_t)
            waitp("vector", out_t.get(pb0 - 1))
            waitp("vector", out_t.get(pb0 - 2))
            wait("vector", s_ve, b1_ev[(OC_OUT - 1, tc)])
            V(lambda e, t=tc, s=slot, o=oth: e.tensor_scalar(
                out=sc[:, o:o + 2048], in0=sc[:, s:s + 2048],
                scalar1=stv(C_M1 + t), scalar2=stv(C_R1 + t),
                op0=ALU.subtract, op1=ALU.mult))
            V(lambda e, t=tc, o=oth: e.scalar_tensor_tensor(
                out=big[:, t * 2048:t * 2048 + 2048], in0=sc[:, o:o + 2048],
                scalar=1.0, in1=big[:, t * 2048:t * 2048 + 2048],
                op0=ALU.mult, op1=ALU.add, accum_out=stv(C_T)))
            af_t = s_ve.n
            wait("scalar", s_ve, af_t)
            wait("scalar", s_ve, sp_guard[0])
            for j in range(2):
                SC(lambda e, t=tc, j_=j: e.activation(
                    out=ternc2[:, 1024:2048],
                    in_=big[:, t * 2048 + j_ * 1024:t * 2048 + (j_ + 1) * 1024],
                    func=AF.Square, accum_out=stv(56 + j_)))
            sq2_t = s_se.n
            wait("vector", s_se, sq2_t)
            V(lambda e: e.tensor_add(out=stv(C_T + 1), in0=stv(56), in1=stv(57)))
            sp_guard[0] = s_ve.n
            var_t = ln_stats_from(C_T, C_T + 1, EPS1, 1.0 / D)
            ln_sqrt_recip(var_t, EPS1)
            V(lambda e, t=tc, o=oth: e.tensor_scalar(
                out=sc[:, o:o + 2048], in0=big[:, t * 2048:t * 2048 + 2048],
                scalar1=stv(C_M), scalar2=stv(C_R), op0=ALU.subtract, op1=ALU.mult))
            V(lambda e, t=tc, s=slot, o=oth: e.tensor_add(
                out=big[:, t * 2048:t * 2048 + 2048], in0=sc[:, o:o + 2048],
                in1=sc[:, s:s + 2048]))
            p5_done[tc] = s_ve.n
            tern_step(4, "both")

        wait("sync", s_cc, cc2_t)
        go_t = dma("sync", SL, lambda e: e.dma_start(
            out=trow[0:2, 132:133],
            in_=gam_out.ap().rearrange("(p j) -> p j", j=1)))
        waitp("vector", go_t)
        V(lambda e: e.tensor_scalar(out=trow[0:2, 133:134],
                                    in0=trow[0:2, 132:133],
                                    scalar1=0.5 / WCNT, scalar2=None,
                                    op0=ALU.mult))
        V(lambda e: e.tensor_scalar(out=trow[0:2, 134:135],
                                    in0=trow[0:2, 133:134],
                                    scalar1=0.5 * EPS2, scalar2=None,
                                    op0=ALU.add))
        V(lambda e: e.tensor_scalar(out=trow[0:2, 135:136],
                                    in0=trow[0:2, 134:135],
                                    scalar1=-1.0, scalar2=None, op0=ALU.mult))
        g6_t = s_ve.n
        wait("sync", s_ve, g6_t)
        t6a = dma("sync", SL, lambda e: e.dma_start(
            out=tern6.ap().rearrange("(p j) -> p j", j=3),
            in_=trow[0:2, 133:136]))
        waitp("sync", t6a)
        ternbc_rec[0] = dma("sync", SL, lambda e: e.dma_start(
            out=ternbc[:, 0:3],
            in_=tern6[0:3][None, :].broadcast_to([128, 3])))
        ternbc_rec[1] = dma("sync", SL, lambda e: e.dma_start(
            out=ternbc[:, 3:6],
            in_=tern6[3:6][None, :].broadcast_to([128, 3])))

        fc_grp, fc_ev = {}, {}
        pj_grp, pj_ev = {}, {}
        out_t = {}
        ring_rel = {}
        pj_ld_ord = [0]
        gidx_tr = [0]

        def harena(ti):
            a = arnE if ti < 2 else arnF
            return a, (ti % 2) * 8192

        for bi in range(NBLK):
            for tc in range(bi * BT, (bi + 1) * BT):
                emit_p5(tc)
            t2_4s[0] = False
            for ti in range(BT):
                tc = bi * BT + ti
                wait("scalar", s_ve, p5_done[tc])
                waitp("vector", t2_last_use.get("sA"))
                waitp("vector", t2_last_use.get("sB"))
                wait("vector", s_se, t2_sign[0])
                if ti == 0 and bi > 0:
                    waitp("vector", out_t.get(bi * BT - 1))
                    waitp("vector", out_t.get(bi * BT - 2))
                stats_pair(big, tc * 2048)
                var_t = ln_stats_from(C_T, C_T + 1, EPS1, 1.0 / D)
                ln_sqrt_recip(var_t, EPS1)
                V(lambda e, t=tc: e.tensor_scalar(
                    out=sc[:, 0:2048], in0=big[:, t * 2048:t * 2048 + 2048],
                    scalar1=stv(C_M), scalar2=stv(C_R),
                    op0=ALU.subtract, op1=ALU.mult))
                xn2_t = s_ve.n
                wait("scalar", s_ve, xn2_t)
                stats_pair(sc, 0)
                var_t = ln_stats_from(C_T, C_T + 1, EPS2, 1.0 / D)
                ln_sqrt_recip(var_t, EPS2)
                V(lambda e: e.tensor_scalar(
                    out=sc[:, 2048:4096], in0=sc[:, 0:2048], scalar1=stv(C_M),
                    scalar2=stv(C_R), op0=ALU.subtract, op1=ALU.mult))
                V(lambda e: e.reduce_max(out=stv(C_G), in_=sc[:, 2048:4096], axis=AX.X,
                                         apply_absolute_value=True))
                V(lambda e: e.tensor_scalar(out=stv(C_T + 6), in0=stv(C_G),
                                            scalar1=CLIPC, scalar2=None, op0=ALU.mult))
                V(lambda e: e.tensor_sub(out=stv(C_CL), in0=stv(C_G), in1=stv(C_T + 6)))
                V(lambda e: e.tensor_scalar(out=stv(C_NCL), in0=stv(C_CL),
                                            scalar1=-1.0, scalar2=None, op0=ALU.mult))
                V(lambda e: e.tensor_scalar(
                    out=scb[:, 0:2048], in0=sc[:, 2048:4096], scalar1=stv(C_CL),
                    scalar2=stv(C_NCL), op0=ALU.min, op1=ALU.max))
                xh_t = s_ve.n
                wait("tensor", s_ve, xh_t)
                for g in range(2):
                    tr_group(
                        (lambda j, g_=g: scb[:, (g_ * 8 + j) * 128:
                                             (g_ * 8 + j + 1) * 128]),
                        arnA[:, ti * 2048 + g * 1024: ti * 2048 + (g + 1) * 1024])
                tern_step(6, "t2")
            XH_ALL = s_ve.n

            t2_4s[0] = True
            for oc in range(OC_FC):
                if oc >= 2:
                    rw = (s_mm, fc_grp[(bi, oc - 2, BT - 1)])
                elif bi > 0:
                    rw = (s_mm, pj_grp[(bi - 1, 3, BT - 1)])
                else:
                    rw = (s_mm, oc_last[OC_QKV + OC_OUT - 1])
                ncut = 16 * (oc // 2 + 1)
                while st_rec.get(ncut - 1) is None and t2_done[0] < N_T2:
                    t2_step(4)
                for z in range(1, 5):
                    waitp("sync", st_rec.get(ncut - z))
                wait("sync", rw[0], rw[1])
                def ldf(e, o=oc):
                    return e.dma_start(
                        out=arnB[:, (o % 2) * 8192:(o % 2) * 8192 + 4096],
                        in_=wfc_t[o, :, 0:4096])
                def ldf2(e, o=oc):
                    return e.dma_start(
                        out=arnB[:, (o % 2) * 8192 + 4096:(o % 2) * 8192 + 8192],
                        in_=wfc_t[o, :, 4096:8192])
                fcs_t = dma("sync", SG, ldf)
                fcs2_t = dma("sync", SG2, ldf2)

                for ti in range(BT):
                    b = grp[0] % 5
                    waitp("tensor", fcs_t)
                    waitp("tensor", fcs2_t)
                    if oc == 0 and ti == 0:
                        wait("tensor", s_ve, XH_ALL)
                    bank_wait(b)
                    for k in range(KD):
                        def mmf(e, t=ti, k_=k, b_=b, o=oc, last=(k == KD - 1)):
                            inst = e.matmul(
                                pb[b_][:, 0:512],
                                arnA[:, (t * KD + k_) * 128:(t * KD + k_ + 1) * 128],
                                slab_ap(o % 2, k_), start=(k_ == 0), stop=last)
                            if last:
                                inst.then_inc(s_mm.h, 1)
                        T(mmf)
                    fc_grp[(bi, oc, ti)] = s_mm.bump(1)
                    grp[0] += 1
                    wait("scalar", s_mm, fc_grp[(bi, oc, ti)])
                    if bi == 0 and oc == 0 and ti == 0:
                        waitp("scalar", ternbc_rec[0])
                    ha, hoff = harena(ti)
                    def evf(e, b_=b, a=ha, o=hoff + oc * 512, c=oc * BT + ti):
                        return e.activation(out=a[:, o:o + 512], in_=pb[b_][:, 0:512],
                                            func=AF.Gelu, scale=ternbc[:, 0:1],
                                            accum_out=hsum[:, c:c + 1])
                    SC(evf)
                    fc_ev[(bi, oc, ti)] = s_se.n
                    bank_free[b] = (s_se, fc_ev[(bi, oc, ti)])
                    tern_step(2, "t2")

            t2_4s[0] = False
            for ti in range(BT):
                ha, hoff = harena(ti)
                wait("vector", s_se, fc_ev[(bi, OC_FC - 1, ti)])
                waitp("vector", t2_last_use.get("sA"))
                waitp("vector", t2_last_use.get("sB"))
                wait("vector", s_se, t2_sign[0])
                V(lambda e, t=ti: e.reduce_sum(
                    out=stv(C_T),
                    in_=hsum[:, 0:OC_FC * BT].rearrange("p (o t) -> p o t", t=BT)[:, :, t],
                    axis=AX.X))
                wait("scalar", s_ve, sp_guard[0])
                for j in range(8):
                    SC(lambda e, a=ha, o=hoff + j * 1024, j_=j: e.activation(
                        out=ternc2[:, 1024:2048], in_=a[:, o:o + 1024],
                        func=AF.Square, accum_out=stv(52 + j_)))
                sqh_t = s_se.n
                wait("vector", s_se, sqh_t)
                V(lambda e: e.tensor_add(out=stv(52), in0=stv(52), in1=stv(53)))
                V(lambda e: e.tensor_add(out=stv(54), in0=stv(54), in1=stv(55)))
                V(lambda e: e.tensor_add(out=stv(56), in0=stv(56), in1=stv(57)))
                V(lambda e: e.tensor_add(out=stv(58), in0=stv(58), in1=stv(59)))
                V(lambda e: e.tensor_add(out=stv(52), in0=stv(52), in1=stv(54)))
                V(lambda e: e.tensor_add(out=stv(56), in0=stv(56), in1=stv(58)))
                V(lambda e: e.tensor_add(out=stv(C_T + 1), in0=stv(52),
                                         in1=stv(56)))
                sp_guard[0] = s_ve.n
                var_t = ln_stats_from(C_T, C_T + 1, EPS2, 1.0 / DFF)
                ln_sqrt_recip(var_t, EPS2)
                V(lambda e, a=ha, o=hoff: e.tensor_scalar(
                    out=scb[:, 0:8192], in0=a[:, o:o + 8192], scalar1=stv(C_M),
                    scalar2=None, op0=ALU.subtract, op1=ALU.max,
                    accum_out=stv(C_T + 6)))
                V(lambda e, a=ha, o=hoff: e.tensor_scalar(
                    out=scb[:, 0:8192], in0=a[:, o:o + 8192], scalar1=stv(C_M),
                    scalar2=None, op0=ALU.subtract, op1=ALU.min,
                    accum_out=stv(C_T + 7)))
                V(lambda e: e.tensor_scalar(out=stv(C_T + 7), in0=stv(C_T + 7),
                                            scalar1=-1.0, scalar2=None, op0=ALU.mult))
                V(lambda e: e.tensor_max(out=stv(C_T + 6), in0=stv(C_T + 6),
                                         in1=stv(C_T + 7)))
                V(lambda e: e.tensor_tensor(out=stv(C_G), in0=stv(C_T + 6),
                                            in1=stv(C_R), op=ALU.mult))
                V(lambda e: e.tensor_scalar(out=stv(C_T + 7), in0=stv(C_G),
                                            scalar1=CLIPC, scalar2=None, op0=ALU.mult))
                V(lambda e: e.tensor_sub(out=stv(C_CL), in0=stv(C_G), in1=stv(C_T + 7)))
                V(lambda e: e.tensor_scalar(out=stv(C_NCL), in0=stv(C_CL),
                                            scalar1=-1.0, scalar2=None, op0=ALU.mult))
                for q in range(4):
                    V(lambda e, a=ha, o=hoff + q * 2048: e.tensor_scalar(
                        out=scb[:, 0:2048], in0=a[:, o:o + 2048],
                        scalar1=stv(C_M), scalar2=stv(C_R),
                        op0=ALU.subtract, op1=ALU.mult))
                    def hh2(e, a=ha, o=hoff + q * 2048):
                        return e.tensor_scalar(
                            out=a[:, o:o + 2048], in0=scb[:, 0:2048],
                            scalar1=stv(C_CL), scalar2=stv(C_NCL),
                            op0=ALU.min, op1=ALU.max)
                    V(hh2)
                hh_t = s_ve.n
                t2_step(2)
                wait("tensor", s_ve, hh_t)
                for g in range(8):
                    tr_group(
                        (lambda j, a=ha, o=hoff, g_=g: a[:, o + (g_ * 8 + j) * 128:
                                                        o + (g_ * 8 + j + 1) * 128]),
                        ha[:, hoff + g * 1024: hoff + (g + 1) * 1024])
            HH_ALL = s_ve.n

            t2_4s[0] = True
            for ech in range(4):
                for j4 in range(KFF // 4):
                    g_ord = pj_ld_ord[0]
                    pj_ld_ord[0] += 1
                    if g_ord - bi * 64 < 4:
                        wait("sync", s_mm, fc_grp[(bi, OC_FC - 2, BT - 1)])
                    else:
                        rr = ring_rel[g_ord - 4]
                        wait("sync", rr[0], rr[1])
                    ncut = 128 + 64 * (ech // 2 + 1)
                    while st_rec.get(ncut - 1) is None and t2_done[0] < N_T2:
                        t2_step(4)
                    for z in range(1, 5):
                        waitp("sync", st_rec.get(ncut - z))
                    def ldp(e, e_=ech, j_=j4):
                        return e.dma_start(
                            out=arnB[:, (j_ % 4) * 2048:(j_ % 4) * 2048 + 2048]
                            .rearrange("p (g o) -> p g o", g=4),
                            in_=wproj_t[e_, j_ * 4:(j_ + 1) * 4, :, :]
                            .rearrange("g p o -> p g o"))
                    pjld_t = dma("sync", SG, ldp)
                    tern_step(2, "t2")
                    for jj in range(4):
                        j = j4 * 4 + jj
                        if ech == 0 and jj == 0 and j4 % 2 == 0:
                            wait("tensor", s_ve, HH_ALL)
                        if jj == 0:
                            waitp("tensor", pjld_t)
                        if j == 0:
                            for ti in range(BT):
                                bank_wait(ti)
                        for ti in range(BT):
                            def mmp(e, t=ti, j_=j, last_g=(j == KFF - 1),
                                    ring=(jj == 3 and ti == BT - 1 and j != KFF - 1)):
                                a, o = harena(t)
                                inst = e.matmul(
                                    pb[t][:, 0:512],
                                    a[:, o + j_ * 128:o + (j_ + 1) * 128],
                                    arnB[:, (j_ % 16) * 512:(j_ % 16 + 1) * 512],
                                    start=(j_ == 0), stop=last_g)
                                if last_g:
                                    inst.then_inc(s_mm.h, 1)
                                elif ring:
                                    inst.then_inc(s_mp.h, 1)
                            T(mmp)
                            if j == KFF - 1:
                                pj_grp[(bi, ech, ti)] = s_mm.bump(1)
                    if j4 == KFF // 4 - 1:
                        ring_rel[g_ord] = (s_mm, s_mm.n)
                    else:
                        ring_rel[g_ord] = (s_mp, s_mp.bump(1))
                for ti in range(BT):
                    wait("scalar", s_mm, pj_grp[(bi, ech, ti)])
                    if bi == 0 and ech == 0 and ti == 0:
                        waitp("scalar", ternbc_rec[1])
                    def evp(e, t=ti, e_=ech):
                        return e.activation(out=arnA[:, 8192 + t * 2048 + e_ * 512:
                                                     8192 + t * 2048 + (e_ + 1) * 512],
                                            in_=pb[t][:, 0:512], func=AF.Copy,
                                            scale=ternbc[:, 3:4])
                    SC(evp)
                    pj_ev[(bi, ech, ti)] = s_se.n
                    bank_free[ti] = (s_se, pj_ev[(bi, ech, ti)])

            t2_4s[0] = False
            for ti in range(BT):
                tc = bi * BT + ti
                slot = (tc % 2) * 2048
                oth = 2048 - slot
                waitp("vector", out_t.get(tc - 1))
                waitp("vector", t2_last_use.get("sA"))
                waitp("vector", t2_last_use.get("sB"))
                wait("vector", s_se, t2_sign[0])
                wait("scalar", s_se, pj_ev[(bi, 3, ti)])
                stats_pair(arnA, 8192 + ti * 2048)
                var_t = ln_stats_from(C_T, C_T + 1, EPS1, 1.0 / D)
                ln_sqrt_recip(var_t, EPS1)
                V(lambda e, t=ti, o=oth: e.tensor_scalar(
                    out=sc[:, o:o + 2048],
                    in0=arnA[:, 8192 + t * 2048:8192 + t * 2048 + 2048],
                    scalar1=stv(C_M), scalar2=stv(C_R),
                    op0=ALU.subtract, op1=ALU.mult))
                V(lambda e, t=tc, s=slot, o=oth: e.tensor_add(
                    out=sc[:, s:s + 2048], in0=sc[:, o:o + 2048],
                    in1=big[:, t * 2048:t * 2048 + 2048]))
                x3_t = s_ve.n
                wait("sync", s_ve, x3_t)
                out_t[tc] = dma("sync", OU, lambda e, t=tc, s=slot: e.dma_start(
                    out=out[t * 128:(t + 1) * 128, :], in_=sc[:, s:s + 2048]))
                tern_step(2, "t2")

        for semx in OU.sems:
            wait("sync", semx, semx.n)

        @blk.sync
        def _(e):
            for f in OPS["sync"]:
                f(e)

        @blk.gpsimd
        def _(e):
            for f in OPS["gpsimd"]:
                f(e)

        @blk.tensor
        def _(e):
            for f in OPS["tensor"]:
                f(e)

        @blk.vector
        def _(e):
            for f in OPS["vector"]:
                f(e)

        @blk.scalar
        def _(e):
            for f in OPS["scalar"]:
                f(e)

    return nc



def _host_layout(W_in, b_in, W_out, w_fc, w_proj):
    perm = np.concatenate([np.arange(0, 1024, 2), np.arange(1, 1024, 2),
                           np.arange(1024, 2048)])
    order = np.concatenate([4096 + perm, 2048 + perm, 0 + perm])
    winT = np.ascontiguousarray(np.asarray(W_in, dtype=np.float32)[order, :].T)
    binb = np.ascontiguousarray(np.asarray(b_in, dtype=np.float32)[order])
    woutT = np.ascontiguousarray(np.asarray(W_out, dtype=np.float32)[:, perm].T)
    wfcT = np.ascontiguousarray(np.asarray(w_fc, dtype=np.float32).T)
    wprojT = np.ascontiguousarray(np.asarray(w_proj, dtype=np.float32).T)
    return winT, binb, woutT, wfcT, wprojT


def _p1_shard(wfcT, wprojT, c):
    chs = []
    for i in range(16 * c, 16 * c + 16):
        kp, ocq = divmod(i, 8)
        chs.append(wfcT[kp * 128:(kp + 1) * 128, ocq * 1024:(ocq + 1) * 1024])
    for i in range(16 * c, 16 * c + 16):
        r, cq = divmod(i, 2)
        chs.append(wprojT[r * 128:(r + 1) * 128, cq * 1024:(cq + 1) * 1024])
    return np.ascontiguousarray(np.concatenate(chs, axis=0))


def _tables(positions):
    inv = 10000.0 ** (-np.arange(512, dtype=np.float64) / 512.0)
    ang = positions[:, None].astype(np.float64) * inv[None, :]
    c64, s64 = np.cos(ang), np.sin(ang)
    ch = c64.astype(np.float32)
    cl = (c64 - ch.astype(np.float64)).astype(np.float32)
    sh = s64.astype(np.float32)
    sl = (s64 - sh.astype(np.float64)).astype(np.float32)
    return np.ascontiguousarray(
        np.concatenate([ch, cl, sh, sl], axis=1).astype(np.float32))


def kernel(L0_attn_X, L0_attn_W_in, L0_attn_b_in, L0_attn_W_out, L0_attn_b_out,
           L0_ffn_w_fc, L0_ffn_w_proj):
    X = np.ascontiguousarray(np.asarray(L0_attn_X), dtype=np.float32)
    B, S, _ = X.shape
    n_cores = 8
    cpb = n_cores // B
    NT = S // cpb
    n_tc = NT // 128
    winT, binb, woutT, wfcT, wprojT = _host_layout(
        L0_attn_W_in, L0_attn_b_in, L0_attn_W_out, L0_ffn_w_fc, L0_ffn_w_proj)
    boutb = np.ascontiguousarray(np.asarray(L0_attn_b_out, dtype=np.float32))
    identf = np.eye(128, dtype=np.float32)

    if n_tc not in _CACHE:
        _CACHE[n_tc] = build(n_tc)
    nc = _CACHE[n_tc]

    in_maps = []
    for c in range(n_cores):
        b = c // cpb
        s0 = (c % cpb) * NT
        in_maps.append(dict(
            x=np.ascontiguousarray(X[b, s0:s0 + NT, :]),
            winT=winT, binb=binb, woutT=woutT, boutb=boutb,
            wfcT=wfcT, wprojT=wprojT,
            wsh=_p1_shard(wfcT, wprojT, c),
            tab=_tables(np.arange(s0, s0 + NT)),
            identf=identf,
        ))
    res = run_bass_kernel_spmd(nc, in_maps, core_ids=list(range(n_cores)))
    shards = [np.asarray(res.results[c]["out"]) for c in range(n_cores)]
    full = np.concatenate(shards, axis=0).reshape(B, S, D)
    return full
